# version 10
# speedup vs baseline: 1.2217x; 1.0127x over previous
import math
import numpy as np

import concourse.bass as bass
import concourse.mybir as mybir
from concourse.bass_utils import run_bass_kernel_spmd

F32 = mybir.dt.float32
F32R = mybir.dt.float32r
BF16 = mybir.dt.bfloat16
AF = mybir.ActivationFunctionType
ALU = mybir.AluOpType
AX = mybir.AxisListType

ENGINES = ("t", "v", "s", "g", "y")


class Sem:

    def __init__(self, handle, name):
        self.h = handle
        self.name = name
        self.n = 0

    def inc(self, k=1):
        self.n += k
        return (self, self.n, k)


class Prog:

    def __init__(self, nc):
        self.nc = nc
        self.q = {e: [] for e in ENGINES}
        self._sems = []

    def sem(self, name):
        h = self.nc.semaphore(name).__enter__()
        s = Sem(h, name)
        self._sems.append(s)
        return s

    def op(self, eng, fn, waits=(), incs=()):
        self.q[eng].append((fn, tuple(waits), tuple(incs)))

    def barrier_wait(self, eng, *semvals):
        self.q[eng].append((None, tuple(semvals), ()))

    def replay(self, block):
        engmap = dict(t=block.tensor, v=block.vector, s=block.scalar,
                      g=block.gpsimd, y=block.sync)

        def make(engname):
            def body(eng):
                for fn, waits, incs in self.q[engname]:
                    for sem, val in waits:
                        eng.wait_ge(sem.h, val)
                    if fn is None:
                        assert not incs
                        continue
                    instr = fn(eng)
                    for item in incs:
                        sem, _val, k = item if len(item) == 3 else (*item, 1)
                        instr = instr.then_inc(sem.h, k)
            return body

        for name in ENGINES:
            if self.q[name]:
                engmap[name](make(name))


class Cfg:

    def __init__(self, D=1024, S0=8192, NCORE=8):
        self.D = D
        self.S0 = S0
        self.NCORE = NCORE
        self.DC = D // 128


def ts(i, n):
    return slice(i * n, (i + 1) * n)


def ceil_div(a, b):
    return -(-a // b)


def emit_ln_transpose(prog, nc, cfg, x_dram, xnT_dram, stats_dram=None, bufs=None,
                      n_tok=None, panel_tiles=8, out_fn=None):
    D = cfg.D
    DC = cfg.DC
    n_tok = n_tok if n_tok is not None else x_dram.shape[0]
    n_tiles = n_tok // 128
    b = bufs

    dma_in = [prog.sem("p0_in0"), prog.sem("p0_in1")]
    vstat = prog.sem("p0_vs")
    srstd = prog.sem("p0_sr")
    vnorm = prog.sem("p0_vn")
    ttr = prog.sem("p0_tt")
    evac = prog.sem("p0_ev")
    dma_out = prog.sem("p0_out")
    dma_st = prog.sem("p0_stout")

    NBUF = 2
    NPS = 2
    panel = panel_tiles

    for i in range(n_tiles):
        xb = b["x"][i % NBUF]
        waits = [(ttr, DC * (i - NBUF + 1))] if i >= NBUF else []
        dsem = dma_in[i % NBUF]
        prog.op("y", lambda e, xb=xb, i=i: e.dma_start(out=xb[:], in_=x_dram[ts(i, 128), :]),
                waits=waits, incs=[dsem.inc(16)])
        w_in = (dsem, dsem.n)

        P = i % NBUF
        st = b["stats"][P]
        mv = b["mv"][P]
        veps = b["veps"][P]
        r2 = b["r2"][P]
        rstd = b["rstd"][P]
        ngrp = ceil_div(D, 512)
        guard = [(vnorm, i - 1)] if i >= NBUF else []
        for gidx in range(ngrp):
            w = 512 if D >= 512 * (gidx + 1) else D - 512 * gidx
            prog.op("v", lambda e, xb=xb, gidx=gidx, w=w, st=st:
                    e.bn_stats(st[:, 6 * gidx:6 * gidx + 6], xb[:, 512 * gidx:512 * gidx + w]),
                    waits=[w_in] + (guard if gidx == 0 else []), incs=[vstat.inc()])
        prog.op("v", lambda e, st=st, mv=mv: e.bn_aggr(mv[:, 0:2], st[:, 0:6 * ngrp]),
                waits=[(vstat, vstat.n)], incs=[vstat.inc()])
        prog.op("v", lambda e, mv=mv, veps=veps: e.tensor_scalar_add(veps[:, 0:1], mv[:, 1:2], 1e-5),
                waits=[(vstat, vstat.n)], incs=[vstat.inc()])
        prog.op("v", lambda e, veps=veps, r2=r2: e.reciprocal(r2[:, 0:1], veps[:, 0:1]),
                waits=[(vstat, vstat.n)], incs=[vstat.inc()])
        prog.op("s", lambda e, r2=r2, rstd=rstd: e.activation(rstd[:, 0:1], r2[:, 0:1], AF.Sqrt),
                waits=[(vstat, vstat.n)], incs=[srstd.inc()])
        xnb = b["xn"][P]
        prog.op("v", lambda e, xb=xb, xnb=xnb, mv=mv, rstd=rstd:
                e.tensor_scalar(xnb[:], xb[:], mv[:, 0:1], rstd[:, 0:1],
                                ALU.subtract, ALU.mult),
                waits=[(srstd, srstd.n)] + ([(ttr, DC * (i - 1))] if i >= NBUF else []),
                incs=[vnorm.inc()])
        if stats_dram is not None:
            msb = b["ms"][P]
            guard_ms = [(dma_st, dma_st.n)] if i >= NBUF else []
            prog.op("v", lambda e, mv=mv, msb=msb: e.tensor_copy(msb[:, 0:1], mv[:, 0:1]),
                    waits=[(vstat, vstat.n)] + guard_ms, incs=[vstat.inc()])
            prog.op("v", lambda e, rstd=rstd, msb=msb: e.tensor_copy(msb[:, 1:2], rstd[:, 0:1]),
                    waits=[(srstd, srstd.n)], incs=[vstat.inc()])
            prog.op("y", lambda e, i=i, msb=msb: e.dma_start(out=stats_dram[ts(i, 128), :], in_=msb[:, 0:2]),
                    waits=[(vstat, vstat.n)], incs=[dma_st.inc(16)])
        w_norm = (vnorm, vnorm.n)

        pcol = i % panel
        for c in range(DC):
            ps = b["tps"][(i * DC + c) % NPS]
            tw = [w_norm]
            prev = i * DC + c - NPS
            if prev >= 0:
                tw.append((evac, prev + 1))
            prog.op("t", lambda e, xnb=xnb, c=c, ps=ps:
                    e.transpose(ps[:], xnb[:, ts(c, 128)], b["ident"][:]),
                    waits=tw, incs=[ttr.inc()])
            prog.op("s", lambda e, ps=ps, c=c, pcol=pcol:
                    e.copy(b["panel"][c][:, ts(pcol, 128)], ps[:]),
                    waits=[(ttr, ttr.n)], incs=[evac.inc()])
        if (i + 1) % panel == 0 or i == n_tiles - 1:
            ncols = (i % panel + 1) * 128
            col0 = (i - i % panel) * 128
            if out_fn is None:
                odst = lambda c, col0, ncols: xnT_dram[ts(c, 128), col0:col0 + ncols]
            else:
                odst = out_fn
            for c in range(DC):
                prog.op("y", lambda e, c=c, col0=col0, ncols=ncols:
                        e.dma_start(out=odst(c, col0, ncols),
                                    in_=b["panel"][c][:, 0:ncols]),
                        waits=[(evac, evac.n)], incs=[dma_out.inc(16)])
            nd = dma_out.n
            prog.barrier_wait("s", (dma_out, nd))
    return dma_out, dma_st


def alloc_ln_bufs(stack, nc, cfg, name, panel_tiles=8, panel_dt=F32R):
    D = cfg.D
    DC = cfg.DC
    b = {}
    b["x"] = [stack.enter_context(nc.sbuf_tensor(f"{name}_x{k}", [128, D], F32)) for k in range(2)]
    b["xn"] = [stack.enter_context(nc.sbuf_tensor(f"{name}_xn{k}", [128, D], F32)) for k in range(2)]
    for key, w in [("stats", 12), ("mv", 2), ("veps", 1), ("r2", 1), ("rstd", 1), ("ms", 2)]:
        b[key] = [stack.enter_context(nc.sbuf_tensor(f"{name}_{key}{k}", [128, w], F32)) for k in range(2)]
    b["ident"] = stack.enter_context(nc.sbuf_tensor(f"{name}_id", [128, 128], F32))
    b["tps"] = [stack.enter_context(nc.psum_tensor(f"{name}_ps{k}", [128, 128], F32)) for k in range(2)]
    b["panel"] = [stack.enter_context(nc.sbuf_tensor(f"{name}_pn{c}", [128, panel_tiles * 128], panel_dt))
                  for c in range(DC)]
    return b


def mm_acc(prog, ps_ap, pairs, waits=(), incs=(), first_extra=()):
    n = len(pairs)
    for j, (lf, rf) in enumerate(pairs):
        prog.op("t",
                (lambda e, lf=lf, rf=rf, j=j, n=n:
                 e.matmul(ps_ap, lf() if callable(lf) else lf,
                          rf() if callable(rf) else rf,
                          start=(j == 0), stop=(j == n - 1))),
                waits=(tuple(waits) + tuple(first_extra)) if j == 0 else (),
                incs=incs if j == n - 1 else ())


class PsumPool:

    def __init__(self, stack, nc, prog, name, shapes):
        self.tiles = [stack.enter_context(nc.psum_tensor(f"{name}{i}", [128, w], F32))
                      for i, w in enumerate(shapes)]
        self.free_at = [None] * len(shapes)

    def acquire(self, i):
        w = [self.free_at[i]] if self.free_at[i] is not None else []
        return self.tiles[i], w

    def release(self, i, sem, val):
        self.free_at[i] = (sem, val)


def emit_l0_attn(prog, nc, stack, cfg, xnT_dram, qoff_dram, R_dram,
                 wqT_dram, wkT_dram, wvT_dram, wpT_dram, bprojB_dram, identb_dram,
                 S0=None, QT=None, PT=None, VW=None, scale=0.125,
                 xag_kv=None, qrow_dram=None):
    D, DC = cfg.D, cfg.DC
    S0 = S0 or cfg.S0
    QT = QT or 1024
    PT = PT or 1024
    VW = VW or 512
    NP = S0 // PT
    NQ = QT // 128
    KW = min(512, PT)
    KK = PT // KW
    NF = VW // 128
    EW = min(512, D)
    NEH = D // EW

    sb = lambda nm, w, dt=F32: stack.enter_context(nc.sbuf_tensor(nm, [128, w], dt))
    RW = DC * max(PT, QT, D)
    r0 = sb("p2_r0", RW, BF16)
    r1 = sb("p2_r1", RW, BF16)
    wq = r0
    xq = r1
    xp = [r0, r1]
    wk = sb("p2_wk", DC * D, BF16)
    wv = sb("p2_wv", DC * VW, BF16)
    bpj = sb("p2_bp", D)
    identb = sb("p2_id", 128, BF16)
    qT = sb("p2_qT", DC * QT, BF16)
    kT = sb("p2_kT", DC * PT, BF16)
    wp = r0
    vv = sb("p2_v", (PT // 128) * VW, BF16)
    sc = [sb(f"p2_sc{k}", PT) for k in range(2)]
    pr = [sb(f"p2_pr{k}", PT, BF16) for k in range(2)]
    pT = [sb(f"p2_pT{k}", PT, BF16) for k in range(2)]
    acc = sb("p2_acc", NQ * VW)
    out = sb("p2_ou", NQ * VW, BF16)
    Rsb = sb("p2_R", D)
    mrun = sb("p2_m", NQ)
    nm = [sb(f"p2_nm{k}", 4) for k in range(2)]
    lrun = sb("p2_l", NQ)
    sumt = [sb(f"p2_su{k}", 1) for k in range(2)]
    rl = sb("p2_rl", 1)

    ps = PsumPool(stack, nc, prog, "p2ps", [512] * 8)

    dmaw = prog.sem("p2_w")
    dmawg = prog.sem("p2_wg")
    dmaxg = prog.sem("p2_xg")
    dmax = [prog.sem("p2_x0"), prog.sem("p2_x1")]
    tsem = prog.sem("p2_t")
    vsem = prog.sem("p2_v")
    ssem = prog.sem("p2_s")
    dout = prog.sem("p2_o")

    for (dst, src, w) in [(wq, wqT_dram, D), (wk, wkT_dram, D), (wv, wvT_dram, VW)]:
        for c in range(DC):
            prog.op("g", lambda e, dst=dst, src=src, c=c, w=w:
                    e.dma_start(out=dst[:, c * w:(c + 1) * w], in_=src[ts(c, 128), :]),
                    incs=[dmawg.inc(16)])
    prog.op("y", lambda e: e.dma_start(out=bpj[:], in_=bprojB_dram[:, :]), incs=[dmaw.inc(16)])
    prog.op("y", lambda e: e.dma_start(out=identb[:], in_=identb_dram[:, :]), incs=[dmaw.inc(16)])
    w_all = (dmaw, dmaw.n)
    w_allg = (dmawg, dmawg.n)

    prog.op("v", lambda e: e.memset(mrun[:], -1e30), incs=[vsem.inc()])
    prog.op("v", lambda e: e.memset(lrun[:], 0.0), incs=[vsem.inc()])
    prog.op("v", lambda e: e.memset(acc[:], 0.0), incs=[vsem.inc()])
    w_init = (vsem, vsem.n)

    qoffv = {}
    qoff_sb = sb("p2_qo", 1, mybir.dt.int32)
    qrow_sb = sb("p2_qr", 1, mybir.dt.int32)
    prog.op("g", lambda e: e.dma_start(out=qoff_sb[0:1, 0:1], in_=qoff_dram[0:1, 0:1]),
            incs=[dmaxg.inc(16)])
    if qrow_dram is not None:
        prog.op("g", lambda e: e.dma_start(out=qrow_sb[0:1, 0:1], in_=qrow_dram[0:1, 0:1]),
                incs=[dmaxg.inc(16)])
    nql = dmaxg.n

    def qload(e, c):
        if "v" not in qoffv:
            qoffv["v"] = e.value_load(qoff_sb[0:1, 0:1])
            if qrow_dram is not None:
                qoffv["r"] = e.value_load(qrow_sb[0:1, 0:1])
        if xag_kv is None:
            return e.dma_start(out=xq[:, c * QT:(c + 1) * QT],
                               in_=xnT_dram[ts(c, 128), bass.ds(qoffv["v"], QT)])
        return e.dma_start(out=xq[:, c * QT:(c + 1) * QT],
                           in_=xag_kv[bass.ds(qoffv["r"] + c * 128, 128),
                                      bass.ds(qoffv["v"], QT)])

    for c in range(DC):
        prog.op("g", lambda e, c=c: qload(e, c),
                waits=[(dmaxg, nql)] if c == 0 else [], incs=[dmaxg.inc(16)])
    w_xq = (dmaxg, dmaxg.n)

    QW = min(512, QT)
    for m in range(DC):
        for tt in range(QT // QW):
            pi = (m * (QT // QW) + tt) % 2
            tile, fw = ps.acquire(pi)
            mm_acc(prog, tile[:, 0:QW],
                   [(wq[:, c * D + 128 * m: c * D + 128 * m + 128],
                     xq[:, c * QT + QW * tt: c * QT + QW * (tt + 1)])
                    for c in range(DC)],
                   waits=[w_allg, w_xq] + fw, incs=[tsem.inc()])
            prog.op("s", lambda e, tile=tile, m=m, tt=tt:
                    e.copy(qT[:, m * QT + QW * tt: m * QT + QW * (tt + 1)], tile[:, 0:QW]),
                    waits=[(tsem, tsem.n)], incs=[ssem.inc()])
            ps.release(pi, ssem, ssem.n)

    t_qproj = tsem.n
    xp_guard = {}
    for p in range(NP):
        xpb = xp[p % 2]
        xw = [(tsem, xp_guard[p % 2])] if (p % 2) in xp_guard else []
        if p < 2:
            xw = xw + [(tsem, t_qproj)]
        xsem = dmax[p % 2]
        if xag_kv is None:
            xsrc = lambda c, p: xnT_dram[ts(c, 128), p * PT:(p + 1) * PT]
        else:
            HW_ = xag_kv.shape[1]
            NRK = 8
            HALF = S0 // 2

            def xsrc(c, p):
                t0 = p * PT
                r = t0 // HALF
                h = (t0 % HALF) // HW_
                row = (h * NRK + r) * D + c * 128
                col = t0 % HW_
                return xag_kv[row:row + 128, col:col + PT]
        for c in range(DC):
            prog.op("y", lambda e, xpb=xpb, c=c, p=p:
                    e.dma_start(out=xpb[:, c * PT:(c + 1) * PT], in_=xsrc(c, p)),
                    waits=xw if c == 0 else [], incs=[xsem.inc(16)])
        w_xp = (xsem, xsem.n)

        for m in range(DC):
            for tt in range(KK):
                pi = (m * KK + tt) % 2
                tile, fw = ps.acquire(pi)
                mm_acc(prog, tile[:, 0:KW],
                       [(wk[:, c * D + 128 * m: c * D + 128 * m + 128],
                         xpb[:, c * PT + KW * tt: c * PT + KW * (tt + 1)])
                        for c in range(DC)],
                       waits=[w_xp] + fw + ([w_allg] if p == 0 else []), incs=[tsem.inc()])
                prog.op("s", lambda e, tile=tile, m=m, tt=tt:
                        e.copy(kT[:, m * PT + KW * tt: m * PT + KW * (tt + 1)], tile[:, 0:KW]),
                        waits=[(tsem, tsem.n)], incs=[ssem.inc()])
                ps.release(pi, ssem, ssem.n)
        for sub in range(PT // 128):
            pi = 2 + sub % 2
            tile, fw = ps.acquire(pi)
            mm_acc(prog, tile[:, 0:VW],
                   [(xpb[:, c * PT + 128 * sub: c * PT + 128 * (sub + 1)],
                     wv[:, c * VW:(c + 1) * VW])
                    for c in range(DC)],
                   waits=fw + ([w_allg, w_xp] if p == 0 else []), incs=[tsem.inc()])
            prog.op("s", lambda e, tile=tile, sub=sub:
                    e.copy(vv[:, sub * VW:(sub + 1) * VW], tile[:, 0:VW]),
                    waits=[(tsem, tsem.n)], incs=[ssem.inc()])
            ps.release(pi, ssem, ssem.n)
        xp_guard[p % 2] = tsem.n

        qctx = {}

        def emit_S1(qt):
            par = qt % 2
            scp, nmp, sump = sc[par], nm[par], sumt[par]
            w_sc = []
            for kk in range(KK):
                pi = 4 + kk % 2
                tile, fw = ps.acquire(pi)
                mm_acc(prog, tile[:, 0:KW],
                       [(qT[:, m * QT + 128 * qt: m * QT + 128 * (qt + 1)],
                         kT[:, m * PT + KW * kk: m * PT + KW * (kk + 1)])
                        for m in range(DC)],
                       waits=fw + ([w_init] if p == 0 and qt == 0 else []),
                       incs=[tsem.inc()])
                prog.op("v", lambda e, tile=tile, kk=kk, scp=scp, nmp=nmp:
                        e.tensor_scalar(scp[:, KW * kk:KW * (kk + 1)], tile[:, 0:KW],
                                        scale, None, ALU.mult, ALU.max,
                                        accum_out=nmp[:, 2 + kk:3 + kk]),
                        waits=[(tsem, tsem.n)] +
                              ([qctx[("probs", par)]] if (kk == 0 and ("probs", par) in qctx) else []),
                        incs=[vsem.inc()])
                ps.release(pi, vsem, vsem.n)
                w_sc.append((vsem, vsem.n))
            prog.op("v", lambda e, nmp=nmp: e.tensor_tensor(nmp[:, 2:3], nmp[:, 2:3], nmp[:, 3:4], ALU.max)
                    if KK == 2 else e.tensor_copy(nmp[:, 2:3], nmp[:, 2:3]),
                    waits=[w_sc[-1]] + ([w_sc[-2]] if KK >= 2 else []), incs=[vsem.inc()])
            prog.op("v", lambda e, qt=qt, nmp=nmp: e.tensor_tensor(nmp[:, 2:3], nmp[:, 2:3], mrun[:, qt:qt + 1], ALU.max),
                    waits=[(vsem, vsem.n)], incs=[vsem.inc()])
            prog.op("v", lambda e, nmp=nmp: e.tensor_scalar_mul(nmp[:, 0:1], nmp[:, 2:3], -1.0),
                    waits=[(vsem, vsem.n)], incs=[vsem.inc()])
            prog.op("s", lambda e, qt=qt, nmp=nmp: e.activation(nmp[:, 1:2], mrun[:, qt:qt + 1], AF.Exp,
                                                       bias=nmp[:, 0:1]),
                    waits=[(vsem, vsem.n)], incs=[ssem.inc()])
            prog.op("v", lambda e, qt=qt, nmp=nmp: e.tensor_copy(mrun[:, qt:qt + 1], nmp[:, 2:3]),
                    waits=[(ssem, ssem.n)], incs=[vsem.inc()])
            prog.op("s", lambda e, scp=scp, nmp=nmp, sump=sump, par=par: e.activation(
                        pr[par][:, 0:PT], scp[:, 0:PT], AF.Exp,
                        bias=nmp[:, 0:1], accum_out=sump[:, 0:1]),
                    waits=[(vsem, vsem.n)] +
                          ([qctx[("trans", par)]] if ("trans", par) in qctx else []),
                    incs=[ssem.inc()])
            qctx[("probs", par)] = (ssem, ssem.n)
            qctx[(qt, "probs")] = (ssem, ssem.n)

        def emit_S2(qt):
            par = qt % 2
            nmp, sump = nm[par], sumt[par]
            w_probs = qctx.pop((qt, "probs"))
            prog.op("v", lambda e, qt=qt, nmp=nmp, sump=sump: e.scalar_tensor_tensor(
                        lrun[:, qt:qt + 1], lrun[:, qt:qt + 1], nmp[:, 1:2], sump[:, 0:1],
                        ALU.mult, ALU.add),
                    waits=[w_probs], incs=[vsem.inc()])
            ntr = PT // 128
            for h in range(ceil_div(ntr, 4)):
                nj = min(4, ntr - 4 * h)
                pi = 6 + h % 2
                tile, fw = ps.acquire(pi)
                for j in range(nj):
                    prog.op("t", lambda e, tile=tile, h=h, j=j, par=par:
                            e.transpose(tile.bitcast(BF16)[:, 128 * j:128 * (j + 1)],
                                        pr[par][:, 512 * h + 128 * j: 512 * h + 128 * (j + 1)],
                                        identb[:, 0:128]),
                            waits=([w_probs, w_all] + fw) if j == 0 else [],
                            incs=[tsem.inc()] if j == nj - 1 else ())
                prog.op("v", lambda e, tile=tile, h=h, nj=nj, par=par:
                        e.tensor_copy(pT[par][:, 512 * h:512 * h + 128 * nj],
                                      tile.bitcast(BF16)[:, 0:128 * nj]),
                        waits=[(tsem, tsem.n)] +
                              ([qctx[("pv", par)]] if (h == 0 and ("pv", par) in qctx) else []),
                        incs=[vsem.inc()])
                ps.release(pi, vsem, vsem.n)
            w_pT = (vsem, vsem.n)
            qctx[("trans", par)] = (tsem, tsem.n)
            pi = 2 + qt % 2
            tile, fw = ps.acquire(pi)
            mm_acc(prog, tile[:, 0:VW],
                   [(pT[par][:, 128 * kk:128 * (kk + 1)], vv[:, kk * VW:(kk + 1) * VW])
                    for kk in range(PT // 128)],
                   waits=[w_pT] + fw, incs=[tsem.inc()])
            qctx[("pv", par)] = (tsem, tsem.n)
            prog.op("v", lambda e, tile=tile, qt=qt, nmp=nmp: e.scalar_tensor_tensor(
                        acc[:, qt * VW:(qt + 1) * VW], acc[:, qt * VW:(qt + 1) * VW],
                        nmp[:, 1:2], tile[:, 0:VW], ALU.mult, ALU.add),
                    waits=[(tsem, tsem.n)], incs=[vsem.inc()])
            ps.release(pi, vsem, vsem.n)

        for qt in range(NQ):
            emit_S1(qt)
            if qt >= 1:
                emit_S2(qt - 1)
        emit_S2(NQ - 1)
        t_scores_end = tsem.n

    for qt in range(NQ):
        prog.op("v", lambda e, qt=qt: e.reciprocal(rl[:, 0:1], lrun[:, qt:qt + 1]),
                waits=[(vsem, vsem.n)], incs=[vsem.inc()])
        prog.op("v", lambda e, qt=qt: e.tensor_scalar(out[:, qt * VW:(qt + 1) * VW],
                                                      acc[:, qt * VW:(qt + 1) * VW],
                                                      rl[:, 0:1], None, ALU.mult),
                waits=[(vsem, vsem.n)], incs=[vsem.inc()])
    w_out = (vsem, vsem.n)

    wpw = prog.sem("p2_wp")
    for c in range(DC):
        prog.op("g", lambda e, c=c: e.dma_start(out=wp[:, c * D:(c + 1) * D],
                                                in_=wpT_dram[ts(c, 128), :]),
                waits=[(tsem, tsem.n)] if c == 0 else [], incs=[wpw.inc(16)])
    w_wp = (wpw, wpw.n)
    for f in range(NF):
        for eh in range(NEH):
            pi = eh % 2
            tile, fw = ps.acquire(pi)
            mm_acc(prog, tile[:, 0:EW],
                   [(out[:, qt * VW + 128 * f: qt * VW + 128 * (f + 1)],
                     wp[:, qt * D + EW * eh: qt * D + EW * (eh + 1)])
                    for qt in range(NQ)],
                   waits=[w_out, w_wp] + fw, incs=[tsem.inc()])
            prog.op("v", lambda e, tile=tile, eh=eh:
                    e.tensor_tensor(Rsb[:, EW * eh:EW * (eh + 1)], tile[:, 0:EW],
                                    bpj[:, EW * eh:EW * (eh + 1)], ALU.add),
                    waits=[(tsem, tsem.n)], incs=[vsem.inc()])
            ps.release(pi, vsem, vsem.n)
        prog.op("y", lambda e, f=f: e.dma_start(out=R_dram[ts(f, 128), :], in_=Rsb[:, 0:D]),
                waits=[(vsem, vsem.n)], incs=[dout.inc(16)])
        prog.barrier_wait("v", (dout, dout.n))
    return dout


def emit_moe(prog, nc, stack, cfg, xnT_dram, WiS_dram, WoS_dram, rwT_dram, rbB_dram,
             identb_dram, onesb_dram, out_dram, SM=4096, E=8, IS=512, xsrc_fn=None):
    D, DC = cfg.D, cfg.DC
    NT = SM // 512
    NIC = IS // 128
    NEH = ceil_div(D, 512)
    EW = min(512, D)

    sb = lambda nm, w, dt=F32: stack.enter_context(nc.sbuf_tensor(nm, [128, w], dt))
    Wi = sb("p1_wi", E * DC * IS, BF16)
    Wo = sb("p1_wo", E * NIC * D, BF16)
    rw = sb("p1_rw", DC * E, BF16)
    rb = sb("p1_rb", E)
    identb = sb("p1_id", 128, BF16)
    onesb = sb("p1_on", E * 128, BF16)
    xt = [sb(f"p1_x{k}", DC * 512, BF16) for k in range(2)]
    wr = sb("p1_wr", E, BF16)
    wrT = sb("p1_wrT", 512, BF16)
    wb = sb("p1_wb", E * 512)
    h1s = sb("p1_h1", E * NIC * 512, BF16)
    h2o = sb("p1_h2o", D)
    nmax = sb("p1_nm", 1)
    sume = sb("p1_se", 1)
    rsum = sb("p1_rs", 1)

    ps = PsumPool(stack, nc, prog, "p1ps", [512] * 8)

    dw = prog.sem("p1_w")
    dwr = prog.sem("p1_wr")
    dwe = [prog.sem(f"p1_we{k}") for k in range(E)]
    dwy = prog.sem("p1_wy")
    dx = [prog.sem("p1_x0"), prog.sem("p1_x1")]
    tsem = prog.sem("p1_t")
    vsem = prog.sem("p1_v")
    ssem = prog.sem("p1_s")
    dout = prog.sem("p1_o")

    for c in range(DC):
        prog.op("g", lambda en, c=c: en.dma_start(out=rw[:, c * E:(c + 1) * E],
                                                  in_=rwT_dram[ts(c, 128), :]),
                incs=[dwr.inc(16)])
    for e in range(E):
        for c in range(DC):
            prog.op("g", lambda en, e_=e, c=c: en.dma_start(
                        out=Wi[:, (e_ * DC + c) * IS:(e_ * DC + c + 1) * IS],
                        in_=WiS_dram[e_, ts(c, 128), :]),
                    incs=[dwe[e].inc(16)])
        for ic in range(NIC):
            prog.op("g", lambda en, e_=e, ic=ic: en.dma_start(
                        out=Wo[:, (e_ * NIC + ic) * D:(e_ * NIC + ic + 1) * D],
                        in_=WoS_dram[e_, ts(ic, 128), :]),
                    incs=[dw.inc(16)])
    prog.op("y", lambda en: en.dma_start(out=rb[:], in_=rbB_dram[:, :]), incs=[dwy.inc(16)])
    prog.op("y", lambda en: en.dma_start(out=identb[:], in_=identb_dram[:, :]), incs=[dwy.inc(16)])
    prog.op("y", lambda en: en.dma_start(out=onesb[:], in_=onesb_dram[:, :]), incs=[dwy.inc(16)])
    w_all = (dw, dw.n)
    w_allr = (dwr, dwr.n)
    w_we = [(s_, s_.n) for s_ in dwe]
    w_ally = (dwy, dwy.n)

    xb_guard = {}
    wr_guard = None
    for t in range(NT):
        xb = xt[t % 2]
        xsem = dx[t % 2]
        xg = [(tsem, xb_guard[t % 2])] if (t % 2) in xb_guard else []
        if xsrc_fn is None:
            xsf = lambda c, t: xnT_dram[ts(c, 128), t * 512:(t + 1) * 512]
        else:
            xsf = xsrc_fn
        for c in range(DC):
            prog.op("g", lambda en, xb=xb, c=c, t=t: en.dma_start(
                        out=xb[:, c * 512:(c + 1) * 512], in_=xsf(c, t)),
                    waits=xg if c == 0 else [], incs=[xsem.inc(16)])
        w_x = (xsem, xsem.n)

        for sub in range(4):
            tile, fw = ps.acquire(4)
            mm_acc(prog, tile[:, 0:E],
                   [(xb[:, c * 512 + 128 * sub: c * 512 + 128 * (sub + 1)],
                     rw[:, c * E:(c + 1) * E]) for c in range(DC)],
                   waits=[w_allr, w_ally, w_x] + fw, incs=[tsem.inc()])
            prog.op("v", lambda en, tile=tile: en.tensor_tensor(wr[:, 0:E], tile[:, 0:E], rb[:, 0:E], ALU.add),
                    waits=[(tsem, tsem.n)] + ([wr_guard] if wr_guard else []), incs=[vsem.inc()])
            ps.release(4, vsem, vsem.n)
            prog.op("v", lambda en: en.reduce_max(nmax[:, 0:1], wr[:, 0:E], axis=AX.X, negate=True),
                    waits=[(vsem, vsem.n)], incs=[vsem.inc()])
            prog.op("s", lambda en: en.activation(wr[:, 0:E], wr[:, 0:E], AF.Exp,
                                                  bias=nmax[:, 0:1], accum_out=sume[:, 0:1]),
                    waits=[(vsem, vsem.n)], incs=[ssem.inc()])
            prog.op("v", lambda en: en.reciprocal(rsum[:, 0:1], sume[:, 0:1]),
                    waits=[(ssem, ssem.n)], incs=[vsem.inc()])
            prog.op("v", lambda en: en.tensor_scalar(wr[:, 0:E], wr[:, 0:E], rsum[:, 0:1], None, ALU.mult),
                    waits=[(vsem, vsem.n)], incs=[vsem.inc()])
            tile2, fw2 = ps.acquire(5)
            prog.op("t", lambda en, tile2=tile2: en.transpose(
                        tile2.bitcast(BF16)[0:E, 0:128], wr[:, 0:E], identb[:, 0:128]),
                    waits=[(vsem, vsem.n)] + fw2, incs=[tsem.inc()])
            wr_guard = (tsem, tsem.n)
            prog.op("v", lambda en, tile2=tile2, sub=sub: en.tensor_copy(
                        wrT[0:E, 128 * sub:128 * (sub + 1)], tile2.bitcast(BF16)[0:E, 0:128]),
                    waits=[(tsem, tsem.n)], incs=[vsem.inc()])
            ps.release(5, vsem, vsem.n)
        for e in range(E):
            tile2, fw2 = ps.acquire(5)
            prog.op("t", lambda en, tile2=tile2, e=e: en.matmul(
                        tile2[:, 0:512], onesb[0:E, e * 128:(e + 1) * 128], wrT[0:E, 0:512],
                        start=True, stop=True),
                    waits=([(vsem, vsem.n)] if e == 0 else []) + fw2, incs=[tsem.inc()])
            prog.op("v", lambda en, tile2=tile2, e=e: en.tensor_copy(wb[:, e * 512:(e + 1) * 512], tile2[:, 0:512]),
                    waits=[(tsem, tsem.n)], incs=[vsem.inc()])
            ps.release(5, vsem, vsem.n)
        w_wb = (vsem, vsem.n)

        for e in range(E):
            for ic in range(NIC):
                bank = (e * NIC + ic) % 4
                tile, fw = ps.acquire(bank)
                mm_acc(prog, tile[:, 0:512],
                       [(Wi[:, (e * DC + c) * IS + 128 * ic:(e * DC + c) * IS + 128 * (ic + 1)],
                         xb[:, c * 512:(c + 1) * 512])
                        for c in range(DC)],
                       waits=fw + ([w_wb] if e == 0 and ic == 0 else [])
                             + ([w_we[e]] if ic == 0 else []),
                       incs=[tsem.inc()])
                prog.op("v", lambda en, tile=tile, e=e, ic=ic: en.scalar_tensor_tensor(
                            h1s[:, e * NIC * 512 + ic * 512:(e * NIC + ic + 1) * 512],
                            tile[:, 0:512], 0.0, wb[:, e * 512:(e + 1) * 512],
                            ALU.max, ALU.mult),
                        waits=[(tsem, tsem.n)], incs=[vsem.inc()])
                ps.release(bank, vsem, vsem.n)
        w_h1 = (vsem, vsem.n)
        for sub in range(4):
            for eh in range(NEH):
                bank = 4 + (sub * NEH + eh) % 2
                tile, fw = ps.acquire(bank)
                pairs = [(h1s[:, (e * NIC + ic) * 512 + 128 * sub:(e * NIC + ic) * 512 + 128 * (sub + 1)],
                          Wo[:, (e * NIC + ic) * D + EW * eh:(e * NIC + ic) * D + EW * (eh + 1)])
                         for e in range(E) for ic in range(NIC)]
                mm_acc(prog, tile[:, 0:EW], pairs,
                       waits=[w_h1, w_all] + fw, incs=[tsem.inc()])
                prog.op("s", lambda en, eh=eh, tile=tile, sub=sub: en.copy(
                            h2o[:, EW * eh:EW * (eh + 1)], tile[:, 0:EW]),
                        waits=[(tsem, tsem.n)], incs=[ssem.inc()])
                ps.release(bank, ssem, ssem.n)
            prog.op("y", lambda en, t=t, sub=sub: en.dma_start(
                        out=out_dram[t * 512 + 128 * sub: t * 512 + 128 * (sub + 1), :],
                        in_=h2o[:, 0:D]),
                    waits=[(ssem, ssem.n)], incs=[dout.inc(16)])
            prog.barrier_wait("s", (dout, dout.n))
        xb_guard[t % 2] = tsem.n
    return dout


def emit_combine0(prog, nc, stack, cfg, Rall_dram, moe_dram, qoff64_dram, delta_dram,
                  SL=512, NCLS=8, moe_stacked=None):
    D = cfg.D
    NTL = SL // 128
    QP = 128 // NCLS
    QW = Rall_dram.shape[0] // NCLS
    sb = lambda nm, w, dt=F32: stack.enter_context(nc.sbuf_tensor(nm, [128, w], dt))
    Rt = [sb(f"p3_r{k}", D) for k in range(2)]
    mo = [sb(f"p3_m{k}", D) for k in range(2)]
    gt = sb("p3_g", D)
    dl = sb("p3_d", D)
    qoff_sb = sb("p3_qo", 1, mybir.dt.int32)

    dR = [prog.sem("p3_r0"), prog.sem("p3_r1")]
    dM = [prog.sem("p3_m0"), prog.sem("p3_m1")]
    vsem = prog.sem("p3_v")
    ssem = prog.sem("p3_s")
    dout = prog.sem("p3_o")

    prog.op("g", lambda e: e.dma_start(out=qoff_sb[0:1, 0:1], in_=qoff64_dram[0:1, 0:1]),
            incs=[dR[0].inc(16)])
    qv = {}

    def rload(e, i, Rb):
        if "v" not in qv:
            qv["v"] = e.value_load(qoff_sb[0:1, 0:1])
        src = Rall_dram.rearrange("(j q) d -> j q d", j=NCLS)
        return e.dma_start(
            out=Rb[:, :],
            in_=src[:, bass.ds(qv["v"] + QP * i, QP), :])

    for i in range(NTL):
        Rb = Rt[i % 2]
        mb = mo[i % 2]
        rw_ = [(vsem, vsem.n)] if i >= 2 else []
        prog.op("g", lambda e, i=i, Rb=Rb: rload(e, i, Rb),
                waits=([(dR[0], 16)] if i == 0 else []) + rw_, incs=[dR[i % 2].inc(16)])
        if moe_dram is not None:
            prog.op("g", lambda e, i=i, mb=mb: e.dma_start(
                        out=mb[:, :],
                        in_=moe_dram.rearrange("(i q j) d -> i j q d", j=NCLS, q=QP)[i]),
                    waits=[], incs=[dM[i % 2].inc(16)])
        else:
            NC = moe_stacked.shape[0]
            for c in range(NC):
                prog.op("g", lambda e, i=i, mb=mb, c=c: e.dma_start(
                            out=mb[:, :],
                            in_=moe_stacked[c].rearrange("(i q j) d -> i j q d", j=NCLS, q=QP)[i],
                            accum_op=(ALU.add if c > 0 else ALU.bypass)),
                        waits=[(dM[i % 2], dM[i % 2].n)],
                        incs=[dM[i % 2].inc(16)])
        w_r = (dR[i % 2], dR[i % 2].n)
        w_m = (dM[i % 2], dM[i % 2].n)
        prog.op("s", lambda e, mb=mb: e.activation(gt[:], mb[:], AF.Silu),
                waits=[w_m, (vsem, vsem.n)], incs=[ssem.inc()])
        prog.op("v", lambda e, Rb=Rb: e.tensor_mul(dl[:], Rb[:], gt[:]),
                waits=[w_r, (ssem, ssem.n), (dout, dout.n)], incs=[vsem.inc()])
        prog.op("y", lambda e, i=i: e.dma_start(
                    out=delta_dram.rearrange("(i q j) d -> i j q d", j=NCLS, q=QP)[i],
                    in_=dl[:, :]),
                waits=[(vsem, vsem.n)], incs=[dout.inc(16)])
    return dout



def cap(handle, off, dims):
    a = handle[:]
    return bass.AP(tensor=a.tensor, offset=a.offset + off, ap=[list(a.ap[0])] + [list(d) for d in dims])


RWKV_C = 32
RWKV_G = 4


def build_rwkv_consts():
    C, G = RWKV_C, RWKV_G
    P = C * G

    def excl_cumsum_M():
        M = np.zeros((P, P), np.float32)
        for i in range(P):
            M[(i // C) * C:i, i] = 1.0
        return M

    def bdry_M(L):
        Mq = np.zeros((P, P), np.float32)
        Mk = np.zeros((P, P), np.float32)
        for i in range(P):
            c0 = (i // C) * C
            loc = i % C
            blk0 = (loc // L) * L
            half = L // 2
            b = c0 + blk0 + half - 1
            if loc % L >= half:
                Mq[b + 1:i, i] = 1.0
            else:
                Mk[i + 1:b + 1, i] = 1.0
        return Mq, Mk

    mats = [excl_cumsum_M()]
    for L in (32, 16, 8, 4):
        mq, mk = bdry_M(L)
        mats += [mq, mk]
    Mdec = np.zeros((P, G), np.float32)
    for c in range(G):
        Mdec[c * C:(c + 1) * C, c] = 1.0
    MstackN = -np.concatenate(mats + [Mdec], axis=1)
    Mkw = np.zeros((P, P), np.float32)
    for i in range(P):
        Mkw[i + 1:(i // C) * C + C, i] = 1.0
    MkwN = -Mkw
    masks = []
    for L in (32, 16, 8, 4, 2):
        m = np.zeros((C, C), np.float32)
        for i in range(C):
            for j in range(C):
                if i // L == j // L and i % L >= L // 2 and j % L < L // 2:
                    m[j, i] = 1.0
        masks.append(m)
    masks.append(np.eye(C, dtype=np.float32))
    masksT = np.tile(np.concatenate(masks, axis=1), (G, 1))
    return MstackN.astype(np.float32), MkwN.astype(np.float32), masksT.astype(np.float32)


def emit_rwkv_main(prog, nc, stack, cfg, rbuf, kbuf, wbuf, vbuf,
                   mstk_dram, mkw_dram, masks_dram, ucol_dram, identb_dram,
                   obuf_dram, qeDT_dram, sfin_dram,
                   NTOK=4096, NH=4, DK=128, DV=256):
    C, G = RWKV_C, RWKV_G
    P = C * G
    NG = NTOK // P
    KD = NH * DK
    VD = NH * DV
    AW = 9 * 128 + 4

    sb = lambda nm, w, dt=F32, p=128: stack.enter_context(nc.sbuf_tensor(nm, [p, w], dt))
    mstk = sb("p5_ms", AW, F32R)
    mkw = sb("p5_mk", 128, F32R)
    msks = sb("p5_msk", 6 * C)
    ucol = sb("p5_u", NH)
    identb = sb("p5_id", 128, BF16)
    rG = [sb(f"p5_r{k}", KD, BF16) for k in range(2)]
    kG = [sb(f"p5_k{k}", KD, BF16) for k in range(2)]
    wGb = [sb(f"p5_w{k}", KD, F32R) for k in range(2)]
    vG = [sb(f"p5_v{k}", VD, BF16) for k in range(2)]
    argsE = [sb(f"p5_ae{k}", NH * 1152, BF16) for k in range(2)]
    dec4 = [sb(f"p5_d4{k}", NH * G) for k in range(2)]
    kwfE = [sb(f"p5_kwf{k}", KD, BF16) for k in range(2)]
    qT = [sb(f"p5_qT{k}", KD, BF16) for k in range(2)]
    kT = [sb(f"p5_kT{k}", KD, BF16) for k in range(2)]
    ku = [sb(f"p5_ku{k}", KD, BF16) for k in range(2)]
    fac = [{}, {}]
    for k in range(2):
        for nmf in ("qe", "q32", "k32", "q16", "k16", "q8", "k8", "q4", "k4"):
            fac[k][nmf] = sb(f"p5_{nmf}{k}", KD, BF16)
    qeDl = sb("p5_qdl", KD, F32R)
    Slocal = sb("p5_Sl", NH * DV, BF16)
    kw = [sb(f"p5_kwm{k}", KD, BF16) for k in range(2)]
    attnT = [sb(f"p5_at{k}", NH * C, BF16) for k in range(2)]
    tml = sb("p5_tm", NH * C)
    tm2 = sb("p5_tm2", NH * C)
    Ssb = sb("p5_S", NH * DV, F32R)
    Dall = sb("p5_D", NH * (G + 1))
    Dpre = sb("p5_Dp", NH)
    zt = sb("p5_z", G)
    otile = sb("p5_o", VD)
    qeD = sb("p5_qd", KD, BF16)

    ps = PsumPool(stack, nc, prog, "p5ps",
                  [512, 512, 512, 512, 256, 512, 512, 512])

    dcon = prog.sem("p5_c")
    dxy = [prog.sem("p5_g0"), prog.sem("p5_g1")]
    tsem = prog.sem("p5_t")
    vsem = prog.sem("p5_v")
    ssem = prog.sem("p5_s")
    dob = prog.sem("p5_db")
    dqd = prog.sem("p5_dq")
    dsf = prog.sem("p5_df")

    for dst, srcd in [(mstk, mstk_dram), (mkw, mkw_dram), (msks, masks_dram),
                      (ucol, ucol_dram), (identb, identb_dram)]:
        prog.op("y", lambda e, dst=dst, srcd=srcd: e.dma_start(out=dst[:], in_=srcd[:, :]),
                incs=[dcon.inc(16)])
    w_con = (dcon, dcon.n)

    prog.op("v", lambda e: e.memset(Ssb[:].bitcast(F32), 0.0), incs=[vsem.inc()])
    prog.op("v", lambda e: e.memset(Dall[:], 0.0), incs=[vsem.inc()])
    prog.op("v", lambda e: e.memset(Dpre[:], 0.0), incs=[vsem.inc()])
    prog.op("v", lambda e: e.tensor_scalar_add(Dall[:], Dall[:], 1.0),
            waits=[(vsem, vsem.n)], incs=[vsem.inc()])
    prog.op("v", lambda e: e.tensor_scalar_add(Dpre[:], Dpre[:], 1.0),
            waits=[(vsem, vsem.n)], incs=[vsem.inc()])
    prog.op("v", lambda e: e.memset(zt[:], 0.0), incs=[vsem.inc()])
    w_init = (vsem, vsem.n)

    hv = lambda t: t.rearrange("p (h c) -> p h c", h=NH)

    io_guard = {}
    sctx = {}

    def emit_S1(g):
        par = g % 2
        rb, kb, wb, vb = rG[par], kG[par], wGb[par], vG[par]
        qTp, kTp, kup, kwp, dec4p = qT[par], kT[par], ku[par], kw[par], dec4[par]
        aep, kwfp = argsE[par], kwfE[par]
        xs = dxy[par]
        gg = [(tsem, io_guard[par])] if par in io_guard else []
        for (dst, srcd) in [(rb, rbuf), (kb, kbuf), (wb, wbuf), (vb, vbuf)]:
            prog.op("y", lambda e, dst=dst, srcd=srcd, g=g: e.dma_start(
                        out=dst[:], in_=srcd[ts(g, 128), :]),
                    waits=gg, incs=[xs.inc(16)])
            gg = []
        w_x = (xs, xs.n)
        wfacG = [(tsem, io_guard[("fac", par)])] if ("fac", par) in io_guard else []
        wdecG = [(vsem, io_guard[("dec", par)])] if ("dec", par) in io_guard else []

        for h in range(NH):
            wh = wb[:, h * DK:(h + 1) * DK]
            t0, f0 = ps.acquire(0)
            mm_acc(prog, t0[:, 0:512], [(wh, mstk[:, 0:512])],
                   waits=[w_con, w_x] + f0 + ([w_init] if g == 0 and h == 0 else []),
                   incs=[tsem.inc()])
            t1, f1 = ps.acquire(1)
            mm_acc(prog, t1[:, 0:512], [(wh, mstk[:, 512:1024])],
                   waits=f1, incs=[tsem.inc()])
            t2a, f2a = ps.acquire(2)
            mm_acc(prog, t2a[:, 0:AW - 1024], [(wh, mstk[:, 1024:AW])],
                   waits=f2a, incs=[tsem.inc()])
            w_args = (tsem, tsem.n)
            prog.op("s", lambda e, t0=t0, h=h, aep=aep: e.activation(
                        aep[:, h * 1152:h * 1152 + 512], t0[:, 0:512], AF.Exp),
                    waits=[w_args], incs=[ssem.inc()])
            ps.release(0, ssem, ssem.n)
            prog.op("s", lambda e, t1=t1, h=h, aep=aep: e.activation(
                        aep[:, h * 1152 + 512:h * 1152 + 1024], t1[:, 0:512], AF.Exp),
                    waits=[w_args], incs=[ssem.inc()])
            ps.release(1, ssem, ssem.n)
            prog.op("s", lambda e, t2a=t2a, h=h, aep=aep: e.activation(
                        aep[:, h * 1152 + 1024:h * 1152 + 1152], t2a[:, 0:128], AF.Exp),
                    waits=[w_args], incs=[ssem.inc()])
            prog.op("s", lambda e, t2a=t2a, h=h, dec4p=dec4p: e.activation(
                        dec4p[:, h * G:(h + 1) * G], t2a[:, 128:132], AF.Exp),
                    waits=[(ssem, ssem.n)] + (wdecG if h == 0 else []), incs=[ssem.inc()])
            ps.release(2, ssem, ssem.n)
            t2, f2 = ps.acquire(2)
            mm_acc(prog, t2[:, 0:128], [(mkw[:, 0:128], wh)],
                   waits=f2 + ([w_con, w_x] if g == 0 and h == 0 else []), incs=[tsem.inc()])
            prog.op("s", lambda e, t2=t2, h=h, kwfp=kwfp: e.activation(
                        kwfp[:, h * DK:(h + 1) * DK], t2[:, 0:128], AF.Exp),
                    waits=[(tsem, tsem.n)], incs=[ssem.inc()])
            ps.release(2, ssem, ssem.n)
            t3, f3 = ps.acquire(3)
            prog.op("t", lambda e, t3=t3, rb=rb, h=h: e.transpose(
                        t3.bitcast(BF16)[:, 0:128], rb[:, h * DK:(h + 1) * DK], identb[:]),
                    waits=f3 + [w_x, w_con], incs=[tsem.inc()])
            prog.op("t", lambda e, t3=t3, kb=kb, h=h: e.transpose(
                        t3.bitcast(BF16)[:, 128:256], kb[:, h * DK:(h + 1) * DK], identb[:]),
                    incs=[tsem.inc()])
            prog.op("v", lambda e, t3=t3, h=h, qTp=qTp: e.tensor_copy(
                        qTp[:, h * DK:(h + 1) * DK], t3.bitcast(BF16)[:, 0:128]),
                    waits=[(tsem, tsem.n)] + (wfacG if h == 0 else []), incs=[vsem.inc()])
            prog.op("v", lambda e, t3=t3, h=h, kTp=kTp: e.tensor_copy(
                        kTp[:, h * DK:(h + 1) * DK], t3.bitcast(BF16)[:, 128:256]),
                    waits=[(tsem, tsem.n)], incs=[vsem.inc()])
            ps.release(3, vsem, vsem.n)
        w_sall = (ssem, ssem.n)
        w_vtr = (vsem, vsem.n)

        sctx[g] = [w_x, w_sall, w_vtr]

    def emit_S1v(g):
        par = g % 2
        kb = kG[par]
        qTp, kTp, kup, kwp = qT[par], kT[par], ku[par], kw[par]
        aep, kwfp = argsE[par], kwfE[par]
        w_x, w_sall, w_vtr = sctx[g]

        def aeview(off):
            return cap(aep, off, [[1152, NH], [1, 128]])
        prog.op("v", lambda e, kTp=kTp, kup=kup: e.tensor_tensor(hv(kup), hv(kTp),
                    ucol[:, 0:NH].unsqueeze(2).broadcast_to([128, NH, DK]), ALU.mult),
                waits=[w_vtr, w_con, (vsem, vsem.n)], incs=[vsem.inc()])
        pairs = [("qe", qTp, 0), ("q32", qTp, 128), ("k32", kTp, 256),
                 ("q16", qTp, 384), ("k16", kTp, 512), ("q8", qTp, 640),
                 ("k8", kTp, 768), ("q4", qTp, 896), ("k4", kTp, 1024)]
        for (nmf, srct, off) in pairs:
            prog.op("v", lambda e, nmf=nmf, srct=srct, off=off, par=par: e.tensor_tensor(
                        hv(fac[par][nmf]), hv(srct), aeview(off), ALU.mult),
                    waits=[w_sall], incs=[vsem.inc()])
        prog.op("v", lambda e, kb=kb, kwp=kwp, kwfp=kwfp: e.tensor_tensor(
                    hv(kwp), hv(kb), hv(kwfp), ALU.mult),
                waits=[(vsem, vsem.n)], incs=[vsem.inc()])
        sctx[g] = [w_x, w_sall, (vsem, vsem.n)]

    def emit_S2(g):
        par = g % 2
        rb, kb, wb, vb = rG[par], kG[par], wGb[par], vG[par]
        qTp, kTp, kup, kwp, dec4p = qT[par], kT[par], ku[par], kw[par], dec4[par]
        facp = fac[par]
        w_x, w_sall, w_fac = sctx.pop(g)

        t3r, f3r = ps.acquire(3)
        t4r, f4r = ps.acquire(4)
        lv_ops = [(t3r, 0, facp["k32"], facp["q32"]), (t3r, 128, facp["k16"], facp["q16"]),
                  (t3r, 256, facp["k8"], facp["q8"]), (t3r, 384, facp["k4"], facp["q4"]),
                  (t4r, 0, kTp, qTp), (t4r, 128, kup, qTp)]
        nmm = 0
        for (tb, colb, kf, qf) in lv_ops:
            for h in range(NH):
                for c in range(G):
                    first = (nmm == 0)
                    nmm += 1
                    last = (nmm == 6 * NH * G)
                    prog.op("t", lambda e, tb=tb, colb=colb, kf=kf, qf=qf, h=h, c=c: e.matmul(
                                tb[32 * c:32 * c + 32, colb + 32 * h:colb + 32 * h + 32],
                                kf[:, h * DK + 32 * c:h * DK + 32 * (c + 1)],
                                qf[:, h * DK + 32 * c:h * DK + 32 * (c + 1)],
                                start=True, stop=True, tile_position=(0, 32 * c), skip_group_check=True),
                            waits=([w_fac] + f3r + f4r) if first else [],
                            incs=[tsem.inc()] if last else ())
        w_rect = (tsem, tsem.n)

        at = attnT[par]
        atg = [(tsem, io_guard[("at", par)])] if ("at", par) in io_guard else []

        def mview(li):
            return cap(msks, li * C, [[0, NH], [1, C]])
        prog.op("v", lambda e, t3r=t3r: e.tensor_tensor(
                    hv(tml), t3r[:, 0:128].rearrange("p (h c) -> p h c", h=NH),
                    mview(0), ALU.mult),
                waits=[w_rect] + atg, incs=[vsem.inc()])
        for li, (tb, colb) in enumerate([(t3r, 128), (t3r, 256), (t3r, 384),
                                         (t4r, 0), (t4r, 128)], start=1):
            prog.op("v", lambda e, tb=tb, colb=colb, li=li: e.tensor_tensor(
                        hv(tm2), tb[:, colb:colb + 128].rearrange("p (h c) -> p h c", h=NH),
                        mview(li), ALU.mult),
                    waits=[(vsem, vsem.n)], incs=[vsem.inc()])
            prog.op("v", lambda e, li=li, at=at: e.tensor_add(
                        at[:] if li == 5 else tml[:], tml[:], tm2[:]),
                    waits=[(vsem, vsem.n)], incs=[vsem.inc()])
        w_asm = (vsem, vsem.n)

        for h in range(NH):
            prog.op("v", lambda e, h=h, dec4p=dec4p: e.tensor_tensor_scan(
                        Dall[:, h * (G + 1) + 1:h * (G + 1) + 1 + G],
                        dec4p[:, h * G:(h + 1) * G], zt[:, 0:G],
                        Dall[:, h * (G + 1):h * (G + 1) + 1],
                        ALU.mult, ALU.add),
                    waits=[(ssem, w_sall[1])] + ([(vsem, vsem.n)] if h == 0 else []),
                    incs=[vsem.inc()])
        dview = cap(Dall, 0, [[G + 1, NH], [1, G], [0, C]])
        prog.op("v", lambda e, facp=facp: e.tensor_tensor(hv(qeDl), hv(facp["qe"]), dview, ALU.mult),
                waits=[(vsem, vsem.n)], incs=[vsem.inc()])
        w_qedl = (vsem, vsem.n)

        for c in range(G):
            for h in range(NH):
                bank = 5 + h // 2
                col0 = (h % 2) * 256
                tb = ps.tiles[bank]
                if c == 0:
                    fwx = ps.acquire(bank)[1] if h % 2 == 0 else []
                    prog.op("t", lambda e, tb=tb, h=h, col0=col0: e.matmul(
                                tb[:, col0:col0 + DV],
                                qeDl[:, h * DK:(h + 1) * DK],
                                Ssb[:, h * DV:(h + 1) * DV],
                                start=(h % 2 == 0), stop=False, skip_group_check=True),
                            waits=[w_qedl, w_asm] + fwx, incs=())
                prog.op("t", lambda e, tb=tb, c=c, h=h, col0=col0, at=at, vb=vb: e.matmul(
                            tb[32 * c:32 * c + 32, col0:col0 + DV],
                            at[32 * c:32 * c + 32, 32 * h:32 * h + 32],
                            vb[32 * c:32 * c + 32, h * DV:(h + 1) * DV],
                            start=False, stop=(c == 0 and h % 2 == 1),
                            tile_position=(32 * c, 32 * c), skip_group_check=True),
                        waits=[(vsem, scan_w[h])] if (c > 0 and scan_w.get(h)) else [],
                        incs=[tsem.inc()] if c == 0 else ())
                if c > 0:
                    prog.op("t", lambda e, tb=tb, c=c, h=h, col0=col0, facp=facp: e.matmul(
                                tb[32 * c:32 * c + 32, col0:col0 + DV],
                                facp["qe"][:, h * DK + 32 * c:h * DK + 32 * (c + 1)],
                                Slocal[:, h * DV:(h + 1) * DV],
                                start=False, stop=(h % 2 == 1),
                                tile_position=(0, 32 * c), skip_group_check=True),
                            incs=[tsem.inc()])
            scan_w = {}
            for h in range(NH):
                wbank = 7 if h % 2 == 0 else 2
                twk, fww = ps.acquire(wbank)
                prog.op("t", lambda e, twk=twk, c=c, h=h, vb=vb, kwp=kwp: e.matmul(
                            twk[:, 0:DV],
                            kwp[32 * c:32 * c + 32, h * DK:(h + 1) * DK],
                            vb[32 * c:32 * c + 32, h * DV:(h + 1) * DV],
                            start=True, stop=True, tile_position=(32 * c, 0), skip_group_check=True),
                        waits=fww + ([w_fac] if (c == 0 and h == 0) else []),
                        incs=[tsem.inc()])
                if c == 0:
                    prog.op("s", lambda e, twk=twk, h=h: e.copy(
                                Slocal[:, h * DV:(h + 1) * DV], twk[:, 0:DV]),
                            waits=[(tsem, tsem.n)], incs=[ssem.inc()])
                    scan_w[h] = None
                else:
                    prog.op("v", lambda e, twk=twk, c=c, h=h, dec4p=dec4p: e.scalar_tensor_tensor(
                                Slocal[:, h * DV:(h + 1) * DV], Slocal[:, h * DV:(h + 1) * DV],
                                dec4p[:, h * G + c:h * G + c + 1], twk[:, 0:DV],
                                ALU.mult, ALU.add),
                            waits=[(tsem, tsem.n)], incs=[vsem.inc()])
                    scan_w[h] = vsem.n
                ps.release(wbank, vsem if c > 0 else ssem, (vsem if c > 0 else ssem).n)
            if c == 0:
                for h in range(NH):
                    scan_w[h] = None
                prog.barrier_wait("t", (ssem, ssem.n))
                scan_w = {}
        d4view = lambda h: Dall[:, h * (G + 1) + G:h * (G + 1) + G + 1]
        for h in range(NH):
            prog.op("v", lambda e, h=h: e.scalar_tensor_tensor(
                        Ssb[:, h * DV:(h + 1) * DV], Ssb[:, h * DV:(h + 1) * DV],
                        d4view(h), Slocal[:, h * DV:(h + 1) * DV],
                        ALU.mult, ALU.add),
                    waits=[(tsem, tsem.n)] + ([(vsem, vsem.n)] if h == 0 else []),
                    incs=[vsem.inc()])
        w_gupd = (vsem, vsem.n)

        for h in range(NH):
            bank = 5 + h // 2
            col0 = (h % 2) * 256
            prog.op("s", lambda e, bank=bank, h=h, col0=col0: e.copy(
                        otile[:, h * DV:(h + 1) * DV], ps.tiles[bank][:, col0:col0 + DV]),
                    waits=[(tsem, tsem.n)], incs=[ssem.inc()])
        ps.release(5, ssem, ssem.n)
        ps.release(6, ssem, ssem.n)
        ps.release(3, vsem, w_asm[1])
        ps.release(4, vsem, w_asm[1])
        prog.op("y", lambda e, g=g: e.dma_start(out=obuf_dram[ts(g, 128), :], in_=otile[:]),
                waits=[(ssem, ssem.n), (dob, dob.n)], incs=[dob.inc(16)])

        dpview = cap(Dpre, 0, [[1, NH], [0, KD // NH]])
        prog.op("v", lambda e: e.tensor_tensor(hv(qeD), hv(qeDl), dpview, ALU.mult),
                waits=[w_gupd], incs=[vsem.inc()])
        dlG = cap(Dall, G, [[G + 1, NH], [1, 1]])
        prog.op("v", lambda e: e.tensor_tensor(
                    cap(Dpre, 0, [[1, NH], [1, 1]]), cap(Dpre, 0, [[1, NH], [1, 1]]),
                    dlG, ALU.mult),
                waits=[(vsem, vsem.n)], incs=[vsem.inc()])
        prog.op("y", lambda e, g=g: e.dma_start(
                    out=qeDT_dram.rearrange("p (h t) -> p h t", h=NH)[:, :, ts(g, 128)],
                    in_=hv(qeD)),
                waits=[(vsem, vsem.n)], incs=[dqd.inc(16)])
        prog.barrier_wait("v", (dqd, dqd.n))
        prog.barrier_wait("s", (dob, dob.n))
        io_guard[par] = tsem.n
        io_guard[("at", par)] = tsem.n
        io_guard[("fac", par)] = tsem.n
        io_guard[("dec", par)] = vsem.n

    emit_S1(0)
    emit_S1v(0)
    for g in range(NG):
        if g + 1 < NG:
            emit_S1(g + 1)
        emit_S2(g)
        if g + 1 < NG:
            emit_S1v(g + 1)

    prog.op("y", lambda e: e.dma_start(out=sfin_dram[:, :], in_=Ssb[:]),
            waits=[(vsem, vsem.n)], incs=[dsf.inc(16)])
    return [dob, dqd, dsf]


def emit_l1_prep(prog, nc, stack, cfg, x_my, delta_dram, prevrow_dram, gates_dram,
                 wx0T_dram, w2a_dram, mu0c_dram, rWT_dram, kWT_dram, vWT_dram,
                 wAT_dram, wBTa_dram, gAT_dram, gBTa_dram, identb_dram,
                 x1_dram, stats_dram, xn1T_dram, rbuf, kbuf, wbuf, vbuf, gbuf,
                 NTOK=4096, KD=512, VD=1024, GLR=64, PLR=32, PP=1024):
    D, DC = cfg.D, cfg.DC
    NPAN = NTOK // PP
    NTL = PP // 128
    NSUB = PP // 128
    TPW = min(512, PP)

    sb = lambda nm, w, dt=F32, p=128: stack.enter_context(nc.sbuf_tensor(nm, [p, w], dt))
    wx0 = sb("p4_wx0", DC * 5 * PLR, BF16)
    w2a = sb("p4_w2a", 5 * D, BF16, PLR + 1)
    mu0c = sb("p4_mu0", DC)
    rWT = sb("p4_rW", DC * KD, BF16)
    kWT = sb("p4_kW", DC * KD, BF16)
    vWT = sb("p4_vW", DC * VD, BF16)
    wAT = sb("p4_wA", DC * GLR, BF16)
    wBTa = sb("p4_wB", KD, BF16, GLR + 1)
    gAT = sb("p4_gA", DC * GLR, BF16)
    gBTa = sb("p4_gB", VD, BF16, GLR + 1)
    identb = sb("p4_id", 128, BF16)
    gates = sb("p4_gt", 4)
    xt = [sb(f"p4_x{k}", D) for k in range(2)]
    dt_ = [sb(f"p4_d{k}", D) for k in range(2)]
    x1t = [sb(f"p4_x1{k}", D) for k in range(2)]
    xnt = [sb(f"p4_xn{k}", D, BF16) for k in range(2)]
    stats = [sb(f"p4_st{k}", 12) for k in range(2)]
    mv = [sb(f"p4_mv{k}", 2) for k in range(2)]
    rstd = [sb(f"p4_rs{k}", 1) for k in range(2)]
    msb = [sb(f"p4_ms{k}", 2) for k in range(2)]
    xn1T = sb("p4_xT", DC * PP, BF16)
    d1T = sb("p4_dT", DC * PP, BF16)
    carry = sb("p4_ca", DC, BF16)
    ubuf = sb("p4_u", DC * PP, BF16)
    xxg = sb("p4_xx", 5 * PP, BF16, PLR + 1)
    tga = sb("p4_tg", PP, BF16, GLR + 1)
    ppb = [sb(f"p4_pp{k}", 512, BF16) for k in range(2)]
    ppw = [sb(f"p4_pw{k}", 512, F32R) for k in range(2)]
    prv = sb("p4_pv", D, F32, 1)
    prvn = sb("p4_pn", D, BF16, 1)
    pst = sb("p4_ps1", 12, F32, 1)
    pmv = sb("p4_pm1", 2, F32, 1)
    prs = sb("p4_pr1", 1, F32, 1)

    ps = PsumPool(stack, nc, prog, "p4ps", [1024, 1024, 512, 512, 512, 512])

    dwg = prog.sem("p4_wg")
    dwy = prog.sem("p4_wy")
    din = [prog.sem(f"p4_i{k}") for k in range(2)]
    tsem = prog.sem("p4_t")
    vsem = prog.sem("p4_v")
    ssem = prog.sem("p4_s")
    dx1 = prog.sem("p4_dx1")
    dst_ = prog.sem("p4_dst")
    dxT = prog.sem("p4_dxT")
    dpan = prog.sem("p4_dp")
    dpp = [prog.sem("p4_pp0"), prog.sem("p4_pp1")]
    dpw = [prog.sem("p4_pw0s"), prog.sem("p4_pw1s")]

    wloads = [(wx0, wx0T_dram, DC, 5 * PLR), (rWT, rWT_dram, DC, KD),
              (kWT, kWT_dram, DC, KD), (vWT, vWT_dram, DC, VD),
              (wAT, wAT_dram, DC, GLR), (gAT, gAT_dram, DC, GLR)]
    for dst, srcd, nch, wdt in wloads:
        for c in range(nch):
            prog.op("g", lambda e, dst=dst, srcd=srcd, c=c, wdt=wdt: e.dma_start(
                        out=dst[:, c * wdt:(c + 1) * wdt], in_=srcd[ts(c, 128), :]),
                    incs=[dwg.inc(16)])
    for dst, srcd in [(w2a, w2a_dram), (wBTa, wBTa_dram), (gBTa, gBTa_dram)]:
        prog.op("g", lambda e, dst=dst, srcd=srcd: e.dma_start(out=dst[:], in_=srcd[:, :]),
                incs=[dwg.inc(16)])
    for dst, srcd in [(mu0c, mu0c_dram), (gates, gates_dram), (identb, identb_dram)]:
        prog.op("y", lambda e, dst=dst, srcd=srcd: e.dma_start(out=dst[:], in_=srcd[:, :]),
                incs=[dwy.inc(16)])
    prog.op("y", lambda e: e.dma_start(out=prv[:], in_=prevrow_dram[0:1, :]), incs=[dwy.inc(16)])
    w_wg = (dwg, dwg.n)
    w_wy = (dwy, dwy.n)

    prog.op("v", lambda e: e.memset(xxg[PLR:PLR + 1, :].bitcast(F32), 0.0), incs=[vsem.inc()])
    prog.op("v", lambda e: e.tensor_scalar_add(xxg[PLR:PLR + 1, :], xxg[PLR:PLR + 1, :], 1.0),
            waits=[(vsem, vsem.n)], incs=[vsem.inc()])
    prog.op("v", lambda e: e.memset(tga[GLR:GLR + 1, :].bitcast(F32), 0.0),
            waits=[(vsem, vsem.n)], incs=[vsem.inc()])
    prog.op("v", lambda e: e.tensor_scalar_add(tga[GLR:GLR + 1, :], tga[GLR:GLR + 1, :], 1.0),
            waits=[(vsem, vsem.n)], incs=[vsem.inc()])

    prog.op("y", lambda e: e.dma_start(out=xt[0][0:1, 0:D], in_=delta_dram[NTOK - 1:NTOK, :]),
            incs=[din[0].inc(16)])
    prog.op("v", lambda e: e.scalar_tensor_tensor(prv[0:1, 0:D], xt[0][0:1, 0:D],
                gates[0:1, 1:2], prv[0:1, 0:D], ALU.mult, ALU.add),
            waits=[(din[0], din[0].n), w_wy], incs=[vsem.inc()])
    ngrp = ceil_div(D, 512)
    for gx in range(ngrp):
        wv_ = min(512, D - 512 * gx)
        prog.op("v", lambda e, gx=gx, wv_=wv_: e.bn_stats(
                    pst[0:1, 6 * gx:6 * gx + 6], prv[0:1, 512 * gx:512 * gx + wv_]),
                waits=[(vsem, vsem.n)], incs=[vsem.inc()])
    prog.op("v", lambda e: e.bn_aggr(pmv[0:1, 0:2], pst[0:1, 0:6 * ngrp]),
            waits=[(vsem, vsem.n)], incs=[vsem.inc()])
    prog.op("v", lambda e: e.tensor_scalar_add(pmv[0:1, 1:2], pmv[0:1, 1:2], 1e-5),
            waits=[(vsem, vsem.n)], incs=[vsem.inc()])
    prog.op("v", lambda e: e.reciprocal(prs[0:1, 0:1], pmv[0:1, 1:2]),
            waits=[(vsem, vsem.n)], incs=[vsem.inc()])
    prog.op("s", lambda e: e.activation(prs[0:1, 0:1], prs[0:1, 0:1], AF.Sqrt),
            waits=[(vsem, vsem.n)], incs=[ssem.inc()])
    prog.op("v", lambda e: e.tensor_scalar(prvn[0:1, 0:D], prv[0:1, 0:D],
                pmv[0:1, 0:1], prs[0:1, 0:1], ALU.subtract, ALU.mult),
            waits=[(ssem, ssem.n)], incs=[vsem.inc()])
    for c in range(DC):
        tile, fw = ps.acquire(0)
        prog.op("t", lambda e, tile=tile, c=c: e.transpose(
                    tile.bitcast(BF16)[:, 0:1], prvn[0:1, ts(c, 128)], identb[0:1, 0:1]),
                waits=[(vsem, vsem.n), w_wy] + fw, incs=[tsem.inc()])
        prog.op("v", lambda e, tile=tile, c=c: e.tensor_scalar(
                    carry[:, c:c + 1], tile.bitcast(BF16)[:, 0:1],
                    gates[:, 2:3], None, ALU.mult),
                waits=[(tsem, tsem.n)], incs=[vsem.inc()])
        ps.release(0, vsem, vsem.n)
    w_carry = (vsem, vsem.n)

    xguard = {}
    inguard = {0: vsem.n}
    uguard = {}
    for p in range(NPAN):
        for i in range(NTL):
            gi = p * NTL + i
            P2 = i % 2
            xb, db, x1b, xnb = xt[P2], dt_[P2], x1t[P2], xnt[P2]
            iw = [(vsem, inguard[P2])] if P2 in inguard else []
            prog.op("y", lambda e, xb=xb, gi=gi: e.dma_start(out=xb[:], in_=x_my[ts(gi, 128), :]),
                    waits=iw, incs=[din[P2].inc(16)])
            prog.op("y", lambda e, db=db, gi=gi: e.dma_start(out=db[:], in_=delta_dram[ts(gi, 128), :]),
                    incs=[din[P2].inc(16)])
            w_in = (din[P2], din[P2].n)
            prog.op("v", lambda e, xb=xb, db=db, x1b=x1b: e.scalar_tensor_tensor(
                        x1b[:], db[:], gates[:, 0:1], xb[:], ALU.mult, ALU.add),
                    waits=[w_in, w_wy] + ([(dx1, dx1.n)] if P2 in inguard else []),
                    incs=[vsem.inc()])
            prog.op("y", lambda e, x1b=x1b, gi=gi: e.dma_start(out=x1_dram[ts(gi, 128), :], in_=x1b[:]),
                    waits=[(vsem, vsem.n)], incs=[dx1.inc(16)])
            st, m2, rs, ms2 = stats[P2], mv[P2], rstd[P2], msb[P2]
            for gx in range(ngrp):
                wv_ = min(512, D - 512 * gx)
                prog.op("v", lambda e, x1b=x1b, st=st, gx=gx, wv_=wv_: e.bn_stats(
                            st[:, 6 * gx:6 * gx + 6], x1b[:, 512 * gx:512 * gx + wv_]),
                        waits=[(vsem, vsem.n)], incs=[vsem.inc()])
            prog.op("v", lambda e, st=st, m2=m2: e.bn_aggr(m2[:, 0:2], st[:, 0:6 * ngrp]),
                    waits=[(vsem, vsem.n)], incs=[vsem.inc()])
            prog.op("v", lambda e, m2=m2, rs=rs: e.tensor_scalar_add(rs[:, 0:1], m2[:, 1:2], 1e-5),
                    waits=[(vsem, vsem.n)], incs=[vsem.inc()])
            prog.op("v", lambda e, rs=rs: e.reciprocal(rs[:, 0:1], rs[:, 0:1]),
                    waits=[(vsem, vsem.n)], incs=[vsem.inc()])
            prog.op("s", lambda e, rs=rs: e.activation(rs[:, 0:1], rs[:, 0:1], AF.Sqrt),
                    waits=[(vsem, vsem.n)], incs=[ssem.inc()])
            prog.op("v", lambda e, x1b=x1b, xnb=xnb, m2=m2, rs=rs: e.tensor_scalar(
                        xnb[:], x1b[:], m2[:, 0:1], rs[:, 0:1], ALU.subtract, ALU.mult),
                    waits=[(ssem, ssem.n)] + ([(tsem, xguard[P2])] if P2 in xguard else []),
                    incs=[vsem.inc()])
            w_xn = (vsem, vsem.n)
            prog.op("v", lambda e, m2=m2, ms2=ms2: e.tensor_copy(ms2[:, 0:1], m2[:, 0:1]),
                    waits=[(vsem, vsem.n)] + ([(dst_, dst_.n)] if P2 in xguard else []),
                    incs=[vsem.inc()])
            prog.op("v", lambda e, rs=rs, ms2=ms2: e.tensor_copy(ms2[:, 1:2], rs[:, 0:1]),
                    waits=[(vsem, vsem.n)], incs=[vsem.inc()])
            prog.op("y", lambda e, ms2=ms2, gi=gi: e.dma_start(out=stats_dram[ts(gi, 128), :], in_=ms2[:, 0:2]),
                    waits=[(vsem, vsem.n)], incs=[dst_.inc(16)])
            for c in range(DC):
                pb = (i * DC + c) % 2
                tile, fw = ps.acquire(pb)
                prog.op("t", lambda e, tile=tile, xnb=xnb, c=c: e.transpose(
                            tile.bitcast(BF16)[:, 0:128], xnb[:, ts(c, 128)], identb[:]),
                        waits=[w_xn] + fw, incs=[tsem.inc()])
                prog.op("s", lambda e, tile=tile, c=c, i=i: e.copy(
                            xn1T[:, c * PP + 128 * i: c * PP + 128 * (i + 1)],
                            tile.bitcast(BF16)[:, 0:128]),
                        waits=[(tsem, tsem.n)] + ([(dxT, dxT.n)] if (i == 0 and c == 0) else []),
                        incs=[ssem.inc()])
                ps.release(pb, ssem, ssem.n)
            xguard[P2] = tsem.n
            inguard[P2] = vsem.n
        w_xT = (ssem, ssem.n)
        for c in range(DC):
            prog.op("y", lambda e, c=c, p=p: e.dma_start(
                        out=xn1T_dram[ts(c, 128), p * PP:(p + 1) * PP],
                        in_=xn1T[:, c * PP:(c + 1) * PP]),
                    waits=[w_xT] if c == 0 else [], incs=[dxT.inc(16)])

        for c in range(DC):
            prog.op("v", lambda e, c=c: e.tensor_sub(
                        d1T[:, c * PP + 1:c * PP + PP],
                        xn1T[:, c * PP:c * PP + PP - 1],
                        xn1T[:, c * PP + 1:c * PP + PP]),
                    waits=[w_xT, w_carry] + ([(tsem, uguard[0])] if 0 in uguard else []),
                    incs=[vsem.inc()])
            prog.op("v", lambda e, c=c: e.tensor_sub(
                        d1T[:, c * PP:c * PP + 1], carry[:, c:c + 1],
                        xn1T[:, c * PP:c * PP + 1]),
                    waits=[(vsem, vsem.n)], incs=[vsem.inc()])
        prog.op("v", lambda e: e.tensor_copy(carry[:, 0:DC],
                    cap(xn1T, PP - 1, [[PP, DC], [1, 1]])),
                waits=[(vsem, vsem.n)], incs=[vsem.inc()])
        w_d1 = (vsem, vsem.n)

        for c in range(DC):
            prog.op("v", lambda e, c=c: e.scalar_tensor_tensor(
                        ubuf[:, c * PP:(c + 1) * PP], d1T[:, c * PP:(c + 1) * PP],
                        mu0c[:, c:c + 1], xn1T[:, c * PP:(c + 1) * PP],
                        ALU.mult, ALU.add),
                    waits=[w_d1, w_wy] + ([(tsem, uguard[0])] if (c == 0 and 0 in uguard) else []),
                    incs=[vsem.inc()])
        w_u0 = (vsem, vsem.n)
        for n in range(5):
            for tt in range(PP // TPW):
                pb = (n * (PP // TPW) + tt) % 2
                tile, fw = ps.acquire(pb)
                mm_acc(prog, tile[0:PLR, 0:TPW],
                       [(wx0[:, c * 5 * PLR + PLR * n: c * 5 * PLR + PLR * (n + 1)],
                         ubuf[:, c * PP + TPW * tt: c * PP + TPW * (tt + 1)])
                        for c in range(DC)],
                       waits=[w_u0, w_wg] + fw, incs=[tsem.inc()])
                prog.op("s", lambda e, tile=tile, n=n, tt=tt: e.activation(
                            xxg[0:PLR, n * PP + TPW * tt: n * PP + TPW * (tt + 1)],
                            tile[0:PLR, 0:TPW], AF.Tanh),
                        waits=[(tsem, tsem.n)], incs=[ssem.inc()])
                ps.release(pb, ssem, ssem.n)
        w_xx = (ssem, ssem.n)

        paths = [("r", rWT, KD, None, BF16),
                 ("k", kWT, KD, None, BF16),
                 ("v", vWT, VD, None, BF16),
                 ("w", wAT, GLR, (wBTa, KD), F32R),
                 ("g", gAT, GLR, (gBTa, VD), BF16)]
        for pidx, (nmp, WT, WW, low, odt) in enumerate(paths):
            n = {"r": 0, "w": 1, "k": 2, "v": 3, "g": 4}[nmp]
            NTT = PP // TPW
            for c in range(DC):
                pb = c % 2
                tile, fw = ps.acquire(pb)
                for tt in range(NTT):
                    mm_acc(prog, tile[:, TPW * tt:TPW * (tt + 1)],
                           [(w2a[0:PLR + 1, n * D + 128 * c: n * D + 128 * c + 128],
                             xxg[0:PLR + 1, n * PP + TPW * tt: n * PP + TPW * (tt + 1)])],
                           waits=([w_xx, w_wg] + fw) if tt == 0 else [],
                           incs=[tsem.inc()] if tt == NTT - 1 else ())
                prog.op("v", lambda e, tile=tile, c=c: e.tensor_tensor(
                            ubuf[:, c * PP: (c + 1) * PP], tile[:, 0:PP],
                            d1T[:, c * PP: (c + 1) * PP], ALU.mult),
                        waits=[(tsem, tsem.n), (vsem, vsem.n)], incs=[vsem.inc()])
                ps.release(pb, vsem, vsem.n)
                prog.op("v", lambda e, c=c: e.tensor_add(
                            ubuf[:, c * PP: (c + 1) * PP],
                            ubuf[:, c * PP: (c + 1) * PP],
                            xn1T[:, c * PP: (c + 1) * PP]),
                        waits=[(vsem, vsem.n)], incs=[vsem.inc()])
            w_ux = (vsem, vsem.n)
            outd = {"r": rbuf, "k": kbuf, "v": vbuf, "w": wbuf, "g": gbuf}[nmp]
            if low is None:
                CW = min(512, WW)
                for sub in range(NSUB):
                    for kk in range(WW // CW):
                        pb = 2 + (sub * (WW // CW) + kk) % 2
                        pe2 = (sub * (WW // CW) + kk) % 2
                        tile, fw = ps.acquire(pb)
                        mm_acc(prog, tile[:, 0:CW],
                               [(ubuf[:, c * PP + 128 * sub: c * PP + 128 * (sub + 1)],
                                 WT[:, c * WW + CW * kk: c * WW + CW * (kk + 1)])
                                for c in range(DC)],
                               waits=[w_ux, w_wg] + fw, incs=[tsem.inc()])
                        prog.op("s", lambda e, tile=tile, pe2=pe2, CW=CW: e.copy(
                                    ppb[pe2][:, 0:CW], tile[:, 0:CW]),
                                waits=[(tsem, tsem.n), (dpp[pe2], dpp[pe2].n)],
                                incs=[ssem.inc()])
                        ps.release(pb, ssem, ssem.n)
                        prog.op("y", lambda e, pe2=pe2, sub=sub, kk=kk, CW=CW, outd=outd, p=p: e.dma_start(
                                    out=outd[p * PP + 128 * sub: p * PP + 128 * (sub + 1),
                                             CW * kk:CW * (kk + 1)],
                                    in_=ppb[pe2][:, 0:CW]),
                                waits=[(ssem, ssem.n)], incs=[dpp[pe2].inc(16)])
            else:
                lowW, OW = low
                for tt in range(PP // TPW):
                    pb = 2 + tt % 2
                    tile, fw = ps.acquire(pb)
                    mm_acc(prog, tile[0:GLR, 0:TPW],
                           [(WT[:, c * GLR:(c + 1) * GLR],
                             ubuf[:, c * PP + TPW * tt: c * PP + TPW * (tt + 1)])
                            for c in range(DC)],
                           waits=[w_ux, w_wg] + fw, incs=[tsem.inc()])
                    prog.op("s", lambda e, tile=tile, tt=tt: e.activation(
                                tga[0:GLR, TPW * tt:TPW * (tt + 1)], tile[0:GLR, 0:TPW], AF.Tanh),
                            waits=[(tsem, tsem.n)], incs=[ssem.inc()])
                    ps.release(pb, ssem, ssem.n)
                w_tg = (ssem, ssem.n)
                CW = min(512, OW)
                for sub in range(NSUB):
                    for kk in range(OW // CW):
                        pb = 2 + (sub * (OW // CW) + kk) % 2
                        pe2 = (sub * (OW // CW) + kk) % 2
                        tile, fw = ps.acquire(pb)
                        mm_acc(prog, tile[:, 0:CW],
                               [(tga[0:GLR + 1, 128 * sub:128 * (sub + 1)],
                                 lowW[0:GLR + 1, CW * kk:CW * (kk + 1)])],
                               waits=[w_tg, w_wg] + fw, incs=[tsem.inc()])
                        if nmp == "w":
                            prog.op("s", lambda e, tile=tile, pe2=pe2, CW=CW:
                                    e.activation(ppw[pe2][:, 0:CW], tile[:, 0:CW], AF.Exp),
                                    waits=[(tsem, tsem.n), (dpw[pe2], dpw[pe2].n)],
                                    incs=[ssem.inc()])
                        else:
                            prog.op("s", lambda e, tile=tile, pe2=pe2, CW=CW:
                                    e.copy(ppb[pe2][:, 0:CW], tile[:, 0:CW]),
                                    waits=[(tsem, tsem.n), (dpp[pe2], dpp[pe2].n)],
                                    incs=[ssem.inc()])
                        ps.release(pb, ssem, ssem.n)
                        buf = ppw if nmp == "w" else ppb
                        dsem2 = dpw if nmp == "w" else dpp
                        prog.op("y", lambda e, pe2=pe2, sub=sub, kk=kk, CW=CW, outd=outd, p=p, buf=buf: e.dma_start(
                                    out=outd[p * PP + 128 * sub: p * PP + 128 * (sub + 1),
                                             CW * kk:CW * (kk + 1)],
                                    in_=buf[pe2][:, 0:CW]),
                                waits=[(ssem, ssem.n)], incs=[dsem2[pe2].inc(16)])
        uguard[0] = tsem.n
    return [dx1, dst_, dxT, dpan, dpp[0], dpp[1], dpw[0], dpw[1]]



def emit_final(prog, nc, stack, cfg, obuf, qeDT_dram, shalf_dram, hgate_dram,
               gbuf, lnab_dram, oWT_dram, x1_dram, stats_dram, xn1T_dram,
               w1s_dram, w2T_dram, rw2T_dram, lng2c_dram, b2rep_dram, rbB_dram,
               uT_dram, vN_dram, gammaB_dram, identb_dram, onesf_dram,
               out_dram, NTOK=4096, NH=4, DV=256, NB=512, II=4096,
               shalf_dyn=None):
    D, DC = cfg.D, cfg.DC
    VD = NH * DV
    VC = VD // 128
    NIC = II // 128
    TW5 = min(512, NTOK)
    NT5 = NTOK // TW5
    NSUB = TW5 // 128
    NS = NTOK // 128
    JC = ceil_div(NB, 128)
    DH = min(512, D)
    NDH = D // DH
    ngo = ceil_div(VD, 512)

    sb = lambda nm, w, dt=F32, p=128: stack.enter_context(nc.sbuf_tensor(nm, [p, w], dt))
    W2T = sb("p6_w2", NIC * D, BF16)
    w1p = [sb(f"p6_w1{k}", DC * 128, BF16) for k in range(3)]
    H1T = sb("p6_h1", NIC * TW5, BF16)
    oWT = sb("p6_ow", DC * D, BF16)
    UhT = sb("p6_uh", DC * NB, BF16)
    Vh = sb("p6_vh", JC * D, BF16)
    Wg2 = sb("p6_wg", DC * NB, BF16)
    rb2B = sb("p6_rb", NB)
    lnab = sb("p6_ln", 2 * VD, BF16)
    gammaB = sb("p6_gm", D, BF16)
    b2s = sb("p6_b2", DC * 128, BF16)
    lng2c = sb("p6_lg", DC)
    identb = sb("p6_id", 128, BF16)
    onesf = sb("p6_of", 128, BF16)
    Shb = sb("p6_sh", NH * DV, BF16)
    hg = sb("p6_hg", 1)
    xnt5 = sb("p6_xt", DC * TW5, BF16)
    qesl = sb("p6_qe", NH * 128, BF16)
    x1t = sb("p6_x1", D)
    stt_ = sb("p6_st", 2)
    gt_ = sb("p6_gt", VD, BF16)
    osb = sb("p6_os", VD)
    og = [sb(f"p6_og{k}", VD, BF16) for k in range(2)]
    ogT = sb("p6_ogT", VC * 128, BF16)
    s1sb = [sb(f"p6_s1{k}", D) for k in range(2)]
    hl = [sb(f"p6_hl{k}", NB) for k in range(2)]
    st2 = sb("p6_st2", 12)
    mv2 = sb("p6_mv2", 2)
    rs2 = sb("p6_rs2", 1)
    alpha = sb("p6_al", NB)
    spl = sb("p6_sp", NB)
    v8 = sb("p6_v8", 8)
    zf = sb("p6_zf", 3)
    beta = [sb(f"p6_be{k}", NB, BF16) for k in range(2)]
    d1s = sb("p6_d1", NB, BF16)
    d1T = sb("p6_dT", JC * 128, BF16)
    fsb = sb("p6_fs", D)
    x2t = sb("p6_x2", D)

    ps = PsumPool(stack, nc, prog, "p6ps", [512] * 8)

    dwg = prog.sem("p6_wg")
    dwy = prog.sem("p6_wy")
    dw1 = [prog.sem(f"p6_w1s{k}") for k in range(3)]
    dxt = prog.sem("p6_xts")
    dio = prog.sem("p6_io")
    dst2 = prog.sem("p6_ist")
    dqe = prog.sem("p6_iqe")
    dx1 = prog.sem("p6_dx1")
    tsem = prog.sem("p6_t")
    vsem = prog.sem("p6_v")
    ssem = prog.sem("p6_s")
    dout = prog.sem("p6_o")

    for c in range(NIC):
        prog.op("g", lambda e, c=c: e.dma_start(out=W2T[:, c * D:(c + 1) * D],
                                                in_=w2T_dram[ts(c, 128), :]),
                incs=[dwg.inc(16)])
    for c in range(DC):
        prog.op("g", lambda e, c=c: e.dma_start(out=oWT[:, c * D:(c + 1) * D],
                                                in_=oWT_dram[ts(c, 128), :]),
                incs=[dwg.inc(16)])
        prog.op("g", lambda e, c=c: e.dma_start(out=UhT[:, c * NB:(c + 1) * NB],
                                                in_=uT_dram[ts(c, 128), :]),
                incs=[dwg.inc(16)])
        prog.op("g", lambda e, c=c: e.dma_start(out=Wg2[:, c * NB:(c + 1) * NB],
                                                in_=rw2T_dram[ts(c, 128), :]),
                incs=[dwg.inc(16)])
        prog.op("g", lambda e, c=c: e.dma_start(out=b2s[:, ts(c, 128)],
                                                in_=b2rep_dram[ts(c, 128), :]),
                incs=[dwg.inc(16)])
    for c in range(JC):
        ph = min(128, NB - 128 * c)
        prog.op("g", lambda e, c=c, ph=ph: e.dma_start(out=Vh[0:ph, c * D:(c + 1) * D],
                                                       in_=vN_dram[128 * c:128 * c + ph, :]),
                incs=[dwg.inc(16)])
    prog.op("g", lambda e: e.dma_start(out=lnab[:], in_=lnab_dram[:, :]), incs=[dwg.inc(16)])
    prog.op("g", lambda e: e.dma_start(out=gammaB[:], in_=gammaB_dram[:, :]), incs=[dwg.inc(16)])
    for dst, srcd in [(lng2c, lng2c_dram), (identb, identb_dram),
                      (onesf, onesf_dram), (hg, hgate_dram)]:
        prog.op("y", lambda e, dst=dst, srcd=srcd: e.dma_start(out=dst[:], in_=srcd[:, :]),
                incs=[dwy.inc(16)])
    prog.op("y", lambda e: e.dma_start(out=rb2B[:], in_=rbB_dram[:, :]), incs=[dwy.inc(16)])
    if shalf_dyn is None:
        prog.op("y", lambda e: e.dma_start(out=osb[:, 0:VD], in_=shalf_dram[:, :]),
                incs=[dwy.inc(16)])
        selv = None
    else:
        sfall_ap, selv_dram = shalf_dyn
        sball = H1T
        selv = sb("p6_sv", 8)
        for r in range(8):
            prog.op("g", lambda e, r=r: e.dma_start(
                        out=sball[:, r * VD:(r + 1) * VD],
                        in_=sfall_ap.bitcast(F32)[ts(r, 128), :]),
                    incs=[dwg.inc(16)])
        prog.op("y", lambda e: e.dma_start(out=selv[:], in_=selv_dram[:, :]),
                incs=[dwy.inc(16)])
    w_wg = (dwg, dwg.n)
    w_wy = (dwy, dwy.n)

    if shalf_dyn is None:
        prog.op("v", lambda e: e.tensor_scalar(Shb[:], osb[:, 0:VD], hg[:, 0:1], None, ALU.mult),
                waits=[w_wy], incs=[vsem.inc()])
    else:
        prog.op("v", lambda e: e.memset(osb[:, 0:VD], 0.0),
                waits=[w_wy, w_wg], incs=[vsem.inc()])
        for r in range(8):
            prog.op("v", lambda e, r=r: e.scalar_tensor_tensor(
                        osb[:, 0:VD], sball[:, r * VD:(r + 1) * VD],
                        selv[:, r:r + 1], osb[:, 0:VD], ALU.mult, ALU.add),
                    waits=[(vsem, vsem.n)], incs=[vsem.inc()])
        prog.op("v", lambda e: e.tensor_copy(Shb[:], osb[:, 0:VD]),
                waits=[(vsem, vsem.n)], incs=[vsem.inc()])
    w_sh = (vsem, vsem.n)

    t4, f4 = ps.acquire(4)
    mm_acc(prog, t4[:, 0:NB],
           [(b2s[:, ts(c, 128)], Wg2[:, c * NB:(c + 1) * NB]) for c in range(DC)],
           waits=[w_wg] + f4, incs=[tsem.inc()])
    prog.op("v", lambda e, t4=t4: e.tensor_add(rb2B[:, 0:NB], rb2B[:, 0:NB], t4[:, 0:NB]),
            waits=[(tsem, tsem.n), w_wy, (vsem, vsem.n)], incs=[vsem.inc()])
    ps.release(4, vsem, vsem.n)
    for c in range(DC):
        prog.op("v", lambda e, c=c: e.tensor_scalar(Wg2[:, c * NB:(c + 1) * NB],
                                                    Wg2[:, c * NB:(c + 1) * NB],
                                                    lng2c[:, c:c + 1], None, ALU.mult),
                waits=[(vsem, vsem.n)], incs=[vsem.inc()])
    usq0 = 8 * VD if shalf_dyn is not None else 0
    for c in range(DC):
        prog.op("v", lambda e, c=c: e.tensor_mul(H1T[:, c * NB:(c + 1) * NB],
                                                 UhT[:, c * NB:(c + 1) * NB],
                                                 UhT[:, c * NB:(c + 1) * NB]),
                waits=[(vsem, vsem.n)] if c == 0 else [(vsem, vsem.n)], incs=[vsem.inc()])
    t5_, f5 = ps.acquire(5)
    mm_acc(prog, t5_[:, 0:NB],
           [(onesf[:, 0:128], H1T[:, c * NB:(c + 1) * NB]) for c in range(DC)],
           waits=[(vsem, vsem.n)] + f5, incs=[tsem.inc()])
    prog.op("s", lambda e, t5_=t5_: e.activation(spl[:, 0:NB], t5_[:, 0:NB], AF.Sqrt),
            waits=[(tsem, tsem.n)], incs=[ssem.inc()])
    ps.release(5, ssem, ssem.n)
    prog.op("v", lambda e: e.tensor_scalar_max(spl[:, 0:NB], spl[:, 0:NB], 1e-6),
            waits=[(ssem, ssem.n)], incs=[vsem.inc()])
    prog.op("v", lambda e: e.reciprocal(spl[:, 0:NB], spl[:, 0:NB]),
            waits=[(vsem, vsem.n)], incs=[vsem.inc()])
    for c in range(DC):
        prog.op("v", lambda e, c=c: e.tensor_tensor(UhT[:, c * NB:(c + 1) * NB],
                                                    UhT[:, c * NB:(c + 1) * NB],
                                                    spl[:, 0:NB], ALU.mult),
                waits=[(vsem, vsem.n)], incs=[vsem.inc()])
    for c in range(JC):
        ph = min(128, NB - 128 * c)
        prog.op("v", lambda e, c=c, ph=ph: e.tensor_mul(osb[0:ph, 0:D], Vh[0:ph, c * D:(c + 1) * D],
                                                        Vh[0:ph, c * D:(c + 1) * D]),
                waits=[(vsem, vsem.n), w_sh], incs=[vsem.inc()])
        prog.op("v", lambda e, ph=ph: e.reduce_sum(rs2[0:ph, 0:1], osb[0:ph, 0:D], axis=AX.X),
                waits=[(vsem, vsem.n)], incs=[vsem.inc()])
        prog.op("s", lambda e, ph=ph: e.activation(rs2[0:ph, 0:1], rs2[0:ph, 0:1], AF.Sqrt),
                waits=[(vsem, vsem.n)], incs=[ssem.inc()])
        prog.op("v", lambda e, ph=ph: e.tensor_scalar_max(rs2[0:ph, 0:1], rs2[0:ph, 0:1], 1e-6),
                waits=[(ssem, ssem.n)], incs=[vsem.inc()])
        prog.op("v", lambda e, ph=ph: e.reciprocal(rs2[0:ph, 0:1], rs2[0:ph, 0:1]),
                waits=[(vsem, vsem.n)], incs=[vsem.inc()])
        prog.op("v", lambda e, c=c, ph=ph: e.tensor_scalar(Vh[0:ph, c * D:(c + 1) * D],
                                                           Vh[0:ph, c * D:(c + 1) * D],
                                                           rs2[0:ph, 0:1], None, ALU.mult),
                waits=[(vsem, vsem.n)], incs=[vsem.inc()])
    w_pre = (vsem, vsem.n)

    w1g = {}
    iog = {"os": (vsem, vsem.n)}
    hlG = {}
    d1G = {}
    ogG = {}
    h2G = {}

    actx = {}

    def emit_A_t(s):
        prog.op("y", lambda e, s=s: e.dma_start(out=stt_[:], in_=stats_dram[ts(s, 128), :]),
                waits=[iog["st"]] if "st" in iog else [], incs=[dst2.inc(16)])
        w_st = (dst2, dst2.n)
        qv = qeDT_dram.rearrange("p (h t) -> p h t", h=NH)
        prog.op("y", lambda e, qv=qv, s=s: e.dma_start(
                    out=qesl[:].rearrange("p (h t) -> p h t", h=NH),
                    in_=qv[:, :, ts(s, 128)]),
                waits=[iog["qe"]] if "qe" in iog else [], incs=[dqe.inc(16)])
        prog.op("y", lambda e, s=s: e.dma_start(out=osb[:, 0:VD], in_=obuf[ts(s, 128), :]),
                waits=[iog["os"]], incs=[dio.inc(16)])
        prog.op("y", lambda e, s=s: e.dma_start(out=gt_[:], in_=gbuf[ts(s, 128), :]),
                waits=[iog["gt"]] if "gt" in iog else [], incs=[dio.inc(16)])
        w_io = (dio, dio.n)
        t0, f0 = ps.acquire(0)
        mm_acc(prog, t0[:, 0:NB],
               [(xnt5[:, c * TW5 + 128 * (s % NSUB): c * TW5 + 128 * (s % NSUB + 1)],
                 Wg2[:, c * NB:(c + 1) * NB]) for c in range(DC)],
               waits=[(dxt, dxt.n), w_pre] + f0, incs=[tsem.inc()])
        w_z = (tsem, tsem.n)
        cw = []
        for hp in range(ceil_div(NH, 2)):
            pb = 1 + hp
            tile, fw = ps.acquire(pb)
            nh2 = min(2, NH - 2 * hp)
            for hh in range(nh2):
                h = 2 * hp + hh
                prog.op("t", lambda e, tile=tile, h=h, hh=hh: e.matmul(
                            tile[:, hh * DV:(hh + 1) * DV],
                            qesl[:, ts(h, 128)], Shb[:, h * DV:(h + 1) * DV],
                            start=True, stop=True),
                        waits=([w_sh, (dqe, dqe.n)] + fw) if hh == 0 else [],
                        incs=[tsem.inc()] if hh == nh2 - 1 else ())
            cw.append((tile, (tsem, tsem.n)))
        iog["qe"] = (tsem, tsem.n)
        t6, f6 = ps.acquire(6)
        mm_acc(prog, t6[:, 0:NB],
               [(xnt5[:, c * TW5 + 128 * (s % NSUB): c * TW5 + 128 * (s % NSUB + 1)],
                 UhT[:, c * NB:(c + 1) * NB]) for c in range(DC)],
               waits=[(dxt, dxt.n), w_pre] + f6, incs=[tsem.inc()])
        w_hl = (tsem, tsem.n)
        hb = hl[s % 2]
        prog.op("s", lambda e, t6=t6, hb=hb: e.copy(hb[:, 0:NB], t6[:, 0:NB]),
                waits=[w_hl] + ([hlG[(s - 2, "r")]] if (s - 2, "r") in hlG else []),
                incs=[ssem.inc()])
        ps.release(6, ssem, ssem.n)
        hlG[s] = (ssem, ssem.n)
        actx[s] = (t0, cw, w_z, w_io, w_st)
        iog["at_t"] = (tsem, tsem.n)

    def emit_A_v(s):
        t0, cw, w_z, w_io, w_st = actx.pop(s)
        for hp, (tile, wmm) in enumerate(cw):
            wseg = min(2 * DV, VD - hp * 2 * DV)
            prog.op("v", lambda e, tile=tile, hp=hp, wseg=wseg: e.tensor_add(
                        osb[:, hp * 2 * DV:hp * 2 * DV + wseg],
                        osb[:, hp * 2 * DV:hp * 2 * DV + wseg], tile[:, 0:wseg]),
                    waits=[wmm, w_io, (vsem, vsem.n)], incs=[vsem.inc()])
            ps.release(1 + hp, vsem, vsem.n)
        prog.op("v", lambda e: e.tensor_mul(rs2[:, 0:1], stt_[:, 1:2], stt_[:, 1:2]),
                waits=[w_st, (vsem, vsem.n)], incs=[vsem.inc()])
        iog["st"] = (vsem, vsem.n)
        prog.op("v", lambda e: e.reciprocal(mv2[:, 0:1], rs2[:, 0:1]),
                waits=[(vsem, vsem.n)], incs=[vsem.inc()])
        prog.op("v", lambda e: e.tensor_scalar_add(mv2[:, 0:1], mv2[:, 0:1], -1e-5),
                waits=[(vsem, vsem.n)], incs=[vsem.inc()])
        prog.op("v", lambda e: e.tensor_mul(mv2[:, 1:2], mv2[:, 0:1], rs2[:, 0:1]),
                waits=[(vsem, vsem.n)], incs=[vsem.inc()])
        prog.op("v", lambda e: e.tensor_scalar_add(mv2[:, 1:2], mv2[:, 1:2], 1e-5),
                waits=[(vsem, vsem.n)], incs=[vsem.inc()])
        prog.op("v", lambda e: e.reciprocal(rs2[:, 0:1], mv2[:, 1:2]),
                waits=[(vsem, vsem.n)], incs=[vsem.inc()])
        prog.op("s", lambda e: e.activation(rs2[:, 0:1], rs2[:, 0:1], AF.Sqrt),
                waits=[(vsem, vsem.n)], incs=[ssem.inc()])
        prog.op("v", lambda e, t0=t0: e.scalar_tensor_tensor(
                    alpha[:, 0:NB], t0[:, 0:NB], rs2[:, 0:1], rb2B[:, 0:NB],
                    ALU.mult, ALU.add),
                waits=[w_z, (ssem, ssem.n)], incs=[vsem.inc()])
        ps.release(0, vsem, vsem.n)
        prog.op("v", lambda e: e.tensor_scalar(alpha[:, 0:NB], alpha[:, 0:NB],
                                               10.0, -10.0, ALU.min, ALU.max),
                waits=[(vsem, vsem.n)], incs=[vsem.inc()])
        prog.op("s", lambda e: e.activation(spl[:, 0:NB], alpha[:, 0:NB], AF.Exp),
                waits=[(vsem, vsem.n)], incs=[ssem.inc()])
        prog.op("s", lambda e: e.activation(alpha[:, 0:NB], spl[:, 0:NB], AF.Ln, bias=1.0),
                waits=[(ssem, ssem.n)], incs=[ssem.inc()])
        prog.barrier_wait("v", (ssem, ssem.n))
        prog.op("v", lambda e: e.max(v8[:, 0:8], alpha[:, 0:NB]),
                waits=[(vsem, vsem.n)], incs=[vsem.inc()])
        prog.op("v", lambda e: e.tensor_scalar(spl[:, 0:NB], alpha[:, 0:NB],
                                               v8[:, 7:8], None, ALU.is_ge),
                waits=[(vsem, vsem.n)], incs=[vsem.inc()])
        prog.op("v", lambda e: e.tensor_mul(alpha[:, 0:NB], alpha[:, 0:NB], spl[:, 0:NB]),
                waits=[(vsem, vsem.n)], incs=[vsem.inc()])
        prog.op("v", lambda e: e.reduce_sum(zf[:, 0:1], alpha[:, 0:NB], axis=AX.X),
                waits=[(vsem, vsem.n)], incs=[vsem.inc()])
        prog.op("s", lambda e: e.activation(zf[:, 1:2], zf[:, 0:1], AF.Tanh),
                waits=[(vsem, vsem.n)], incs=[ssem.inc()])
        prog.op("v", lambda e: e.tensor_scalar_add(zf[:, 0:1], zf[:, 0:1], 1e-6),
                waits=[(ssem, ssem.n)], incs=[vsem.inc()])
        prog.op("v", lambda e: e.reciprocal(zf[:, 0:1], zf[:, 0:1]),
                waits=[(vsem, vsem.n)], incs=[vsem.inc()])
        prog.op("v", lambda e: e.tensor_mul(zf[:, 2:3], zf[:, 0:1], zf[:, 1:2]),
                waits=[(vsem, vsem.n)], incs=[vsem.inc()])
        bpar = beta[s % 2]
        prog.op("v", lambda e, bpar=bpar: e.tensor_scalar(bpar[:, 0:NB], alpha[:, 0:NB],
                                                          zf[:, 2:3], None, ALU.mult),
                waits=[(vsem, vsem.n)], incs=[vsem.inc()])
        for gx in range(ngo):
            wv_ = min(512, VD - 512 * gx)
            prog.op("v", lambda e, gx=gx, wv_=wv_: e.bn_stats(
                        st2[:, 6 * gx:6 * gx + 6], osb[:, 512 * gx:512 * gx + wv_]),
                    waits=[(vsem, vsem.n)], incs=[vsem.inc()])
        prog.op("v", lambda e: e.bn_aggr(mv2[:, 0:2], st2[:, 0:6 * ngo]),
                waits=[(vsem, vsem.n)], incs=[vsem.inc()])
        prog.op("v", lambda e: e.tensor_scalar_add(rs2[:, 0:1], mv2[:, 1:2], 1e-5),
                waits=[(vsem, vsem.n)], incs=[vsem.inc()])
        prog.op("v", lambda e: e.reciprocal(rs2[:, 0:1], rs2[:, 0:1]),
                waits=[(vsem, vsem.n)], incs=[vsem.inc()])
        prog.op("s", lambda e: e.activation(rs2[:, 0:1], rs2[:, 0:1], AF.Sqrt),
                waits=[(vsem, vsem.n)], incs=[ssem.inc()])
        prog.op("v", lambda e: e.tensor_scalar(osb[:, 0:VD], osb[:, 0:VD], mv2[:, 0:1],
                                               rs2[:, 0:1], ALU.subtract, ALU.mult),
                waits=[(ssem, ssem.n)], incs=[vsem.inc()])
        prog.op("v", lambda e: e.tensor_mul(osb[:, 0:VD], osb[:, 0:VD], lnab[:, 0:VD]),
                waits=[(vsem, vsem.n)], incs=[vsem.inc()])
        prog.op("v", lambda e: e.tensor_add(osb[:, 0:VD], osb[:, 0:VD], lnab[:, VD:2 * VD]),
                waits=[(vsem, vsem.n)], incs=[vsem.inc()])
        prog.op("s", lambda e: e.activation(gt_[:], gt_[:], AF.Silu),
                waits=[w_io, (ssem, ssem.n)], incs=[ssem.inc()])
        opar = og[s % 2]
        prog.op("v", lambda e, opar=opar: e.tensor_mul(opar[:], osb[:, 0:VD], gt_[:]),
                waits=[(ssem, ssem.n), (vsem, vsem.n)] +
                      ([ogG[(s - 2, "t")]] if (s - 2, "t") in ogG else []),
                incs=[vsem.inc()])
        iog["os"] = (vsem, vsem.n)
        iog["gt"] = (vsem, vsem.n)
        ogG[(s, "w")] = (vsem, vsem.n)

    def emit_d1s(s):
        bpar = beta[s % 2]
        hb = hl[s % 2]
        wd = [hlG[s], ogG[(s, "w")], (vsem, vsem.n)]
        if s - 1 in d1G:
            wd.append(d1G[s - 1])
        prog.op("v", lambda e, hb=hb, bpar=bpar: e.tensor_tensor(
                    d1s[:, 0:NB], hb[:, 0:NB], bpar[:, 0:NB], ALU.mult),
                waits=wd, incs=[vsem.inc()])
        hlG[(s - 0, "r")] = (vsem, vsem.n)
        hlG[(s, "rd")] = (vsem, vsem.n)
        hlG[(s, "r")] = (vsem, vsem.n)

    def emit_Btensor(s):
        opar = og[s % 2]
        t3, f3 = ps.acquire(3)
        for c in range(VC):
            prog.op("t", lambda e, t3=t3, c=c, opar=opar: e.transpose(
                        t3.bitcast(BF16)[:, ts(c, 128)], opar[:, ts(c, 128)], identb[:]),
                    waits=([ogG[(s, "w")], w_wy] + f3) if c == 0 else [],
                    incs=[tsem.inc()])
        ogG[(s, "t")] = (tsem, tsem.n)
        prog.op("s", lambda e, t3=t3: e.copy(ogT[:, 0:VC * 128], t3.bitcast(BF16)[:, 0:VC * 128]),
                waits=[(tsem, tsem.n)], incs=[ssem.inc()])
        ps.release(3, ssem, ssem.n)
        w_ogT = (ssem, ssem.n)
        spar = s1sb[s % 2]
        for hf in range(NDH):
            pb = 4 + hf
            tile, fw = ps.acquire(pb)
            mm_acc(prog, tile[:, 0:DH],
                   [(ogT[:, ts(c, 128)], oWT[:, c * D + hf * DH: c * D + hf * DH + DH])
                    for c in range(VC)],
                   waits=[w_ogT, w_wg] + fw, incs=[tsem.inc()])
            prog.op("s", lambda e, tile=tile, hf=hf, spar=spar: e.copy(
                        spar[:, hf * DH:(hf + 1) * DH], tile[:, 0:DH]),
                    waits=[(tsem, tsem.n)] +
                          ([h2G[(s - 2, "s1r")]] if (s - 2, "s1r") in h2G else []),
                    incs=[ssem.inc()])
            ps.release(pb, ssem, ssem.n)
        h2G[(s, "s1")] = (ssem, ssem.n)
        t3, f3 = ps.acquire(3)
        for c in range(JC):
            ph = min(128, NB - 128 * c)
            prog.op("t", lambda e, t3=t3, c=c, ph=ph: e.transpose(
                        t3.bitcast(BF16)[0:ph, ts(c, 128)], d1s[:, 128 * c:128 * c + ph],
                        identb[:]),
                    waits=([hlG[(s, "rd")]] + f3) if c == 0 else [], incs=[tsem.inc()])
        d1G[s] = (tsem, tsem.n)
        prog.op("s", lambda e, t3=t3: e.copy(d1T[:, 0:JC * 128], t3.bitcast(BF16)[:, 0:JC * 128]),
                waits=[(tsem, tsem.n)], incs=[ssem.inc()])
        ps.release(3, ssem, ssem.n)
        w_d1T = (ssem, ssem.n)
        tiles_dh = []
        for hf in range(NDH):
            pbd = 4 + hf
            tiled, fwd = ps.acquire(pbd)
            mm_acc(prog, tiled[:, 0:DH],
                   [((lambda c=c, ph=min(128, NB - 128 * c): d1T[0:ph, ts(c, 128)])(),
                     Vh[0:min(128, NB - 128 * c), c * D + hf * DH: c * D + hf * DH + DH])
                    for c in range(JC)],
                   waits=[w_d1T, w_pre] + fwd, incs=[tsem.inc()])
            w_dyn = (tsem, tsem.n)
            pbh = 7 if hf == 0 else 6
            tileh, fwh = ps.acquire(pbh)
            mm_acc(prog, tileh[:, 0:DH],
                   [(H1T[:, ic * TW5 + 128 * (s % NSUB): ic * TW5 + 128 * (s % NSUB + 1)],
                     W2T[:, ic * D + hf * DH: ic * D + hf * DH + DH])
                    for ic in range(NIC)],
                   waits=[iog["h1"], w_wg] + fwh, incs=[tsem.inc()])
            tiles_dh.append((tiled, w_dyn, pbd, tileh, (tsem, tsem.n), pbh))
        h2G[(s, "mm")] = tiles_dh

    def emit_combinev(s):
        prog.op("y", lambda e, s=s: e.dma_start(out=x1t[:], in_=x1_dram[ts(s, 128), :]),
                waits=[iog["x1"]] if "x1" in iog else [], incs=[dx1.inc(16)])
        w_x1 = (dx1, dx1.n)
        spar = s1sb[s % 2]
        for hf, (tiled, w_dyn, pbd, tileh, w_h2, pbh) in enumerate(h2G.pop((s, "mm"))):
            prog.op("v", lambda e, tiled=tiled, hf=hf: e.tensor_tensor(
                        fsb[:, hf * DH:(hf + 1) * DH], tiled[:, 0:DH],
                        gammaB[:, hf * DH:(hf + 1) * DH], ALU.mult),
                    waits=[w_dyn, (vsem, vsem.n)], incs=[vsem.inc()])
            ps.release(pbd, vsem, vsem.n)
            prog.op("v", lambda e, tileh=tileh, hf=hf: e.tensor_add(
                        fsb[:, hf * DH:(hf + 1) * DH], fsb[:, hf * DH:(hf + 1) * DH],
                        tileh[:, 0:DH]),
                    waits=[w_h2, (vsem, vsem.n)], incs=[vsem.inc()])
            ps.release(pbh, vsem, vsem.n)
            prog.op("s", lambda e, hf=hf: e.activation(
                        fsb[:, hf * DH:(hf + 1) * DH], fsb[:, hf * DH:(hf + 1) * DH],
                        AF.Silu),
                    waits=[(vsem, vsem.n)], incs=[ssem.inc()])
            prog.op("v", lambda e, hf=hf, spar=spar: e.tensor_tensor(
                        x2t[:, hf * DH:(hf + 1) * DH], spar[:, hf * DH:(hf + 1) * DH],
                        fsb[:, hf * DH:(hf + 1) * DH], ALU.mult),
                    waits=[(ssem, ssem.n), h2G[(s, "s1")], w_x1] +
                          ([(dout, dout.n)] if s > 0 else []),
                    incs=[vsem.inc()])
            prog.op("v", lambda e, hf=hf: e.tensor_add(
                        x2t[:, hf * DH:(hf + 1) * DH], x2t[:, hf * DH:(hf + 1) * DH],
                        x1t[:, hf * DH:(hf + 1) * DH]),
                    waits=[(vsem, vsem.n)], incs=[vsem.inc()])
        iog["x1"] = (vsem, vsem.n)
        h2G[(s, "s1r")] = (vsem, vsem.n)
        iog["odma"] = (s, (vsem, vsem.n))

    def flush_outdma():
        if "odma" in iog:
            s, w = iog.pop("odma")
            prog.op("y", lambda e, s=s: e.dma_start(out=out_dram[ts(s, 128), :], in_=x2t[:]),
                    waits=[w], incs=[dout.inc(16)])

    w1bf = nc.dram_tensor("p6w1bf", [II, D], BF16).ap()
    dwc = prog.sem("p6_wc")
    for b_ in range(4):
        prog.op("g", lambda e, b_=b_: e.dma_start(
                    out=w1bf[b_ * (II // 4):(b_ + 1) * (II // 4), :],
                    in_=w1s_dram[b_ * (II // 4):(b_ + 1) * (II // 4), :]),
                incs=[dwc.inc(16)])
    w_wc = (dwc, dwc.n)

    def emit_H1(t5):
        for isub in range(NIC):
            wp = w1p[isub % 3]
            wsem = dw1[isub % 3]
            wg2_ = [w1g[isub - 3]] if (isub - 3) in w1g else []
            prog.op("g", lambda e, wp=wp, isub=isub: e.dma_start(
                        out=wp[:], in_=w1bf[ts(isub, 128), :]),
                    waits=wg2_ + ([w_wc] if t5 == 0 and isub == 0 else []),
                    incs=[wsem.inc(16)])
            pb = 1 if isub % 2 == 0 else 2
            tile, fw = ps.acquire(pb)
            mm_acc(prog, tile[:, 0:TW5],
                   [(wp[:, ts(c, 128)], xnt5[:, c * TW5:(c + 1) * TW5]) for c in range(DC)],
                   waits=[(wsem, wsem.n), (dxt, dxt.n)] +
                         ([w_pre] if t5 == 0 else []) + fw,
                   incs=[tsem.inc()])
            w1g[isub] = (tsem, tsem.n)
            prog.op("s", lambda e, tile=tile, isub=isub: e.activation(
                        H1T[:, isub * TW5:(isub + 1) * TW5], tile[:, 0:TW5], AF.Gelu),
                    waits=[(tsem, tsem.n)], incs=[ssem.inc()])
            ps.release(pb, ssem, ssem.n)
        iog["h1"] = (ssem, ssem.n)
        w1g.clear()

    for t5 in range(NT5):
        s0 = t5 * NSUB
        for c in range(DC):
            prog.op("g", lambda e, c=c, t5=t5: e.dma_start(
                        out=xnt5[:, c * TW5:(c + 1) * TW5],
                        in_=xn1T_dram[ts(c, 128), t5 * TW5:(t5 + 1) * TW5]),
                    waits=[iog["at_t"]] if (t5 > 0 and c == 0) else [],
                    incs=[dxt.inc(16)])
        for k in range(NSUB):
            s = s0 + k
            if s - 1 >= 0:
                emit_d1s(s - 1)
            if s - 2 >= 0:
                emit_combinev(s - 2)
            emit_A_t(s)
            flush_outdma()
            if s - 1 >= 0:
                emit_Btensor(s - 1)
            emit_A_v(s)
            if k == 0:
                emit_H1(t5)
    emit_d1s(NS - 1)
    emit_combinev(NS - 2)
    flush_outdma()
    emit_Btensor(NS - 1)
    emit_combinev(NS - 1)
    flush_outdma()
    return dout


_B, _S, _D = 4, 8192, 1024
_NC = 8
_NTOK = 4096
_E, _IS = 8, 512
_KD, _VD, _NH, _DV = 512, 1024, 4, 256
_GLR, _PLR = 64, 32
_II = 4096


def _eye16():
    import ml_dtypes
    return np.eye(128, dtype=ml_dtypes.bfloat16)


def _launch(build_fn, in_maps, outs):
    nc = build_fn()
    res = run_bass_kernel_spmd(nc, in_maps, core_ids=list(range(_NC)))
    return [{k: r[k] for k in outs} for r in res.results]


def _mk_dr(nc):
    return lambda nm, shp, dt=F32, kind="ExternalInput": nc.dram_tensor(
        nm, list(shp), dt, kind=kind).ap()


def _build_p0():
    import contextlib
    nc = bass.Bass("TRN2", target_bir_lowering=False, debug=False,
                   detect_race_conditions=False)
    cfg = Cfg(D=_D)
    dr = _mk_dr(nc)
    x = dr("x", (_NTOK, _D))
    ident = dr("ident", (128, 128))
    xnT = dr("xnT", (_D, _NTOK), F32R, kind="ExternalOutput")
    with contextlib.ExitStack() as stack:
        bufs = alloc_ln_bufs(stack, nc, cfg, "ln", panel_tiles=8)
        prog = Prog(nc)
        ldid = prog.sem("ldid")
        prog.op("y", lambda e: e.dma_start(out=bufs["ident"][:], in_=ident[:, :]),
                incs=[ldid.inc(16)])
        prog.barrier_wait("t", (ldid, 16))
        dma_out, dma_st = emit_ln_transpose(prog, nc, cfg, x, xnT, stats_dram=None,
                                            bufs=bufs, n_tok=_NTOK, panel_tiles=8)
        prog.barrier_wait("y", (dma_out, dma_out.n))
        with nc.Block() as block:
            prog.replay(block)
    return nc


def _build_p2():
    import contextlib
    nc = bass.Bass("TRN2", target_bir_lowering=False, debug=False,
                   detect_race_conditions=False)
    cfg = Cfg(D=_D)
    dr = _mk_dr(nc)
    xnT = dr("xnT", (_D, _S), F32R)
    qoff = dr("qoff", (1, 1), mybir.dt.int32)
    wqT = dr("wqT", (_D, _D), F32R)
    wkT = dr("wkT", (_D, _D), F32R)
    wvT = dr("wvT", (_D, 512), F32R)
    wpT = dr("wpT", (_D, _D), F32R)
    bprojB = dr("bprojB", (128, _D))
    identb = dr("identb", (128, 128), BF16)
    R = dr("R", (512, _D), kind="ExternalOutput")
    with contextlib.ExitStack() as stack:
        prog = Prog(nc)
        dout = emit_l0_attn(prog, nc, stack, cfg, xnT, qoff, R,
                            wqT, wkT, wvT, wpT, bprojB, identb,
                            S0=_S, QT=1024, PT=1024, VW=512)
        prog.barrier_wait("y", (dout, dout.n))
        with nc.Block() as block:
            prog.replay(block)
    return nc


def _build_p1():
    import contextlib
    nc = bass.Bass("TRN2", target_bir_lowering=False, debug=False,
                   detect_race_conditions=False)
    cfg = Cfg(D=_D)
    dr = _mk_dr(nc)
    xnT = dr("xnT", (_D, _NTOK), F32R)
    WiS = dr("WiS", (_E, _D, _IS))
    WoS = dr("WoS", (_E, _IS, _D))
    rwT = dr("rwT", (_D, _E))
    rbB = dr("rbB", (128, _E))
    identb = dr("identb", (128, 128), BF16)
    onesb = dr("onesb", (128, _E * 128), BF16)
    out = dr("out", (_NTOK, _D), kind="ExternalOutput")
    with contextlib.ExitStack() as stack:
        prog = Prog(nc)
        dout = emit_moe(prog, nc, stack, cfg, xnT, WiS, WoS, rwT, rbB, identb,
                        onesb, out, SM=_NTOK, E=_E, IS=_IS)
        prog.barrier_wait("y", (dout, dout.n))
        with nc.Block() as block:
            prog.replay(block)
    return nc


def _build_p3():
    import contextlib
    nc = bass.Bass("TRN2", target_bir_lowering=False, debug=False,
                   detect_race_conditions=False)
    cfg = Cfg(D=_D)
    dr = _mk_dr(nc)
    Rall = dr("Rall", (_NTOK, _D))
    moest = dr("moest", (_NC, 512, _D))
    qoff64 = dr("qoff64", (1, 1), mybir.dt.int32)
    delta = dr("delta", (512, _D), kind="ExternalOutput")
    with contextlib.ExitStack() as stack:
        prog = Prog(nc)
        dout = emit_combine0(prog, nc, stack, cfg, Rall, None, qoff64, delta,
                             SL=512, NCLS=8, moe_stacked=moest)
        prog.barrier_wait("y", (dout, dout.n))
        with nc.Block() as block:
            prog.replay(block)
    return nc


def _build_p4():
    import contextlib
    nc = bass.Bass("TRN2", target_bir_lowering=False, debug=False,
                   detect_race_conditions=False)
    cfg = Cfg(D=_D)
    dr = _mk_dr(nc)
    io = {}
    ins = dict(x_my=(_NTOK, _D), delta=(_NTOK, _D), prevrow=(1, _D), gates=(128, 4),
               wx0T=(_D, 5 * _PLR), w2a=(_PLR + 1, 5 * _D), mu0c=(128, _D // 128),
               rWT=(_D, _KD), kWT=(_D, _KD), vWT=(_D, _VD), wAT=(_D, _GLR),
               wBTa=(_GLR + 1, _KD), gAT=(_D, _GLR), gBTa=(_GLR + 1, _VD))
    for nm, shp in ins.items():
        io[nm] = dr(nm, shp)
    io["identb"] = dr("identb", (128, 128), BF16)
    outs = dict(x1=((_NTOK, _D), F32), statso=((_NTOK, 2), F32),
                xn1T=((_D, _NTOK), BF16), rbuf=((_NTOK, _KD), BF16),
                kbuf=((_NTOK, _KD), BF16), wbuf=((_NTOK, _KD), F32R),
                vbuf=((_NTOK, _VD), BF16), gbuf=((_NTOK, _VD), BF16))
    for nm, (shp, dt) in outs.items():
        io[nm] = dr(nm, shp, dt, kind="ExternalOutput")
    with contextlib.ExitStack() as stack:
        prog = Prog(nc)
        ds = emit_l1_prep(prog, nc, stack, cfg, io["x_my"], io["delta"], io["prevrow"],
                          io["gates"], io["wx0T"], io["w2a"], io["mu0c"], io["rWT"],
                          io["kWT"], io["vWT"], io["wAT"], io["wBTa"], io["gAT"],
                          io["gBTa"], io["identb"], io["x1"], io["statso"], io["xn1T"],
                          io["rbuf"], io["kbuf"], io["wbuf"], io["vbuf"], io["gbuf"],
                          NTOK=_NTOK, KD=_KD, VD=_VD, GLR=_GLR, PLR=_PLR, PP=1024)
        prog.barrier_wait("y", *[(d, d.n) for d in ds])
        with nc.Block() as block:
            prog.replay(block)
    return nc


def _build_p5():
    import contextlib
    nc = bass.Bass("TRN2", target_bir_lowering=False, debug=False,
                   detect_race_conditions=False)
    cfg = Cfg(D=_D)
    dr = _mk_dr(nc)
    rbuf = dr("rbuf", (_NTOK, _KD), BF16)
    kbuf = dr("kbuf", (_NTOK, _KD), BF16)
    wbuf = dr("wbuf", (_NTOK, _KD), F32R)
    vbuf = dr("vbuf", (_NTOK, _VD), BF16)
    mstk = dr("mstk", (128, 9 * 128 + 4), F32R)
    mkw = dr("mkw", (128, 128), F32R)
    msks = dr("msks", (128, 6 * 32))
    ucol = dr("ucol", (128, _NH))
    identb = dr("identb", (128, 128), BF16)
    obuf = dr("obuf", (_NTOK, _VD), kind="ExternalOutput")
    qeDT = dr("qeDT", (128, _NH * _NTOK), BF16, kind="ExternalOutput")
    sfin = dr("sfin", (128, _NH * _DV), F32R, kind="ExternalOutput")
    with contextlib.ExitStack() as stack:
        prog = Prog(nc)
        douts = emit_rwkv_main(prog, nc, stack, cfg, rbuf, kbuf, wbuf, vbuf,
                               mstk, mkw, msks, ucol, identb, obuf, qeDT, sfin,
                               NTOK=_NTOK, NH=_NH, DK=128, DV=_DV)
        prog.barrier_wait("y", *[(d, d.n) for d in douts])
        with nc.Block() as block:
            prog.replay(block)
    return nc


def _build_p6():
    import contextlib
    nc = bass.Bass("TRN2", target_bir_lowering=False, debug=False,
                   detect_race_conditions=False)
    cfg = Cfg(D=_D)
    dr = _mk_dr(nc)
    obuf = dr("obuf", (_NTOK, _VD))
    qeDT = dr("qeDT", (128, _NH * _NTOK), BF16)
    shalf = dr("shalf", (128, _VD))
    hgate = dr("hgate", (128, 1))
    gbuf = dr("gbuf", (_NTOK, _VD), BF16)
    lnab = dr("lnab", (128, 2 * _VD))
    oWT = dr("oWT", (_VD, _D))
    x1 = dr("x1", (_NTOK, _D))
    stats = dr("stats", (_NTOK, 2))
    xn1T = dr("xn1T", (_D, _NTOK), BF16)
    w1s = dr("w1s", (_II, _D))
    w2T = dr("w2T", (_II, _D))
    rw2T = dr("rw2T", (_D, 512))
    lng2c = dr("lng2c", (128, _D // 128))
    b2rep = dr("b2rep", (_D, 128))
    rbB = dr("rbB", (128, 512))
    uT = dr("uT", (_D, 512))
    vN = dr("vN", (512, _D))
    gammaB = dr("gammaB", (128, _D))
    identb = dr("identb", (128, 128), BF16)
    onesf = dr("onesf", (128, 128), BF16)
    out = dr("out", (_NTOK, _D), kind="ExternalOutput")
    with contextlib.ExitStack() as stack:
        prog = Prog(nc)
        dout = emit_final(prog, nc, stack, cfg, obuf, qeDT, shalf, hgate,
                          gbuf, lnab, oWT, x1, stats, xn1T, w1s, w2T,
                          rw2T, lng2c, b2rep, rbB, uT, vN, gammaB,
                          identb, onesf, out,
                          NTOK=_NTOK, NH=_NH, DV=_DV, NB=512, II=_II)
        prog.barrier_wait("y", (dout, dout.n))
        with nc.Block() as block:
            prog.replay(block)
    return nc


def kernel(**inputs):
    import ml_dtypes
    g = lambda k: np.ascontiguousarray(np.asarray(inputs[k], np.float32))
    xf = g("L0_attn_x").reshape(_B * _S, _D)
    eye32 = np.eye(128, dtype=np.float32)
    eye16 = _eye16()

    r0 = _launch(_build_p0,
                 [{"x": xf[ts(i, _NTOK)], "ident": eye32} for i in range(_NC)],
                 ["xnT"])
    xn0T = np.ascontiguousarray(np.concatenate(
        [np.asarray(r0[0]["xnT"], np.float32), np.asarray(r0[1]["xnT"], np.float32)], axis=1))

    wqkv = g("L0_attn_w_qkv")
    com2 = {"xnT": xn0T,
            "wqT": np.ascontiguousarray(wqkv[0:1024].T),
            "wkT": np.ascontiguousarray(wqkv[1024:2048].T),
            "wvT": np.ascontiguousarray(wqkv[2048:2560].T),
            "wpT": np.ascontiguousarray(g("L0_attn_w_proj").T),
            "bprojB": np.tile(g("L0_attn_b_proj"), (128, 1)),
            "identb": eye16}
    r2 = _launch(_build_p2,
                 [dict(com2, qoff=np.array([[1024 * i]], np.int32)) for i in range(_NC)],
                 ["R"])
    Rall = np.ascontiguousarray(np.concatenate(
        [np.asarray(r2[i]["R"], np.float32) for i in range(_NC)], axis=0))

    Wi = g("L0_ffn_experts_inter")
    Wo = g("L0_ffn_experts_out")
    sel = np.zeros((128, _E * 128), np.float32)
    for e in range(_E):
        sel[e, e * 128:(e + 1) * 128] = 1.0
    sel = sel.astype(ml_dtypes.bfloat16)
    com1 = {"xnT": np.ascontiguousarray(xn0T[:, 0:_NTOK]),
            "rwT": np.ascontiguousarray(g("L0_ffn_router_w").T),
            "rbB": np.tile(g("L0_ffn_router_b"), (128, 1)),
            "identb": eye16, "onesb": sel}
    r1 = _launch(_build_p1,
                 [dict(com1,
                       WiS=np.ascontiguousarray(Wi[:, :, ts(i, _IS)]),
                       WoS=np.ascontiguousarray(Wo[:, ts(i, _IS), :])) for i in range(_NC)],
                 ["out"])
    moeP = [np.asarray(r1[i]["out"], np.float32) for i in range(_NC)]

    r3 = _launch(_build_p3,
                 [{"Rall": Rall,
                   "moest": np.ascontiguousarray(np.stack([moeP[j][ts(i, 512)] for j in range(_NC)])),
                   "qoff64": np.array([[64 * i]], np.int32)} for i in range(_NC)],
                 ["delta"])
    delta = np.ascontiguousarray(np.concatenate(
        [np.asarray(r3[i]["delta"], np.float32) for i in range(_NC)], axis=0))

    Wx2 = g("L1_attn_Wx2")
    x_bias = g("L1_attn_x_bias")
    w2a = np.zeros((_PLR + 1, 5 * _D), np.float32)
    for n in range(5):
        w2a[:_PLR, n * _D:(n + 1) * _D] = Wx2.reshape(_D, 5, _PLR)[:, n].T
        w2a[_PLR, n * _D:(n + 1) * _D] = x_bias[n]
    wBTa = np.zeros((_GLR + 1, _KD), np.float32)
    wBTa[:_GLR] = g("L1_attn_wB").T
    wBTa[_GLR] = g("L1_attn_wb")
    gBTa = np.zeros((_GLR + 1, _VD), np.float32)
    gBTa[:_GLR] = g("L1_attn_gB").T
    gBTa[_GLR] = g("L1_attn_gb")
    com4 = {"delta": delta,
            "wx0T": np.ascontiguousarray(g("L1_attn_Wx0").T),
            "w2a": w2a,
            "mu0c": np.ascontiguousarray(g("L1_attn_mu0").reshape(_D // 128, 128).T),
            "rWT": np.ascontiguousarray(g("L1_attn_rW").T),
            "kWT": np.ascontiguousarray(g("L1_attn_kW").T),
            "vWT": np.ascontiguousarray(g("L1_attn_vW").T),
            "wAT": np.ascontiguousarray(g("L1_attn_wA").T),
            "wBTa": wBTa,
            "gAT": np.ascontiguousarray(g("L1_attn_gA").T),
            "gBTa": gBTa, "identb": eye16}
    in4 = []
    for i in range(_NC):
        gates = np.zeros((128, 4), np.float32)
        gates[:, 0] = 1.0 if i == 0 else 0.0
        gates[:, 1] = 1.0 if i == 1 else 0.0
        gates[:, 2] = 1.0 if i % 2 == 1 else 0.0
        prevrow = xf[i * _NTOK - 1:i * _NTOK] if i % 2 == 1 else np.zeros((1, _D), np.float32)
        in4.append(dict(com4, x_my=xf[ts(i, _NTOK)], gates=gates,
                        prevrow=np.ascontiguousarray(prevrow)))
    r4 = _launch(_build_p4, in4,
                 ["x1", "statso", "xn1T", "rbuf", "kbuf", "wbuf", "vbuf", "gbuf"])

    MstackN, MkwN, masksT = build_rwkv_consts()
    com5 = {"mstk": MstackN, "mkw": MkwN, "msks": masksT,
            "ucol": np.ascontiguousarray(g("L1_attn_bonus").T), "identb": eye16}
    r5 = _launch(_build_p5,
                 [dict(com5, rbuf=r4[i]["rbuf"], kbuf=r4[i]["kbuf"],
                       wbuf=r4[i]["wbuf"], vbuf=r4[i]["vbuf"]) for i in range(_NC)],
                 ["obuf", "qeDT", "sfin"])

    W1 = g("L1_ffn_W1")
    w1s = np.ascontiguousarray(
        W1.reshape(_II // 128, 128, _D // 128, 128).transpose(0, 3, 2, 1).reshape(_II, _D))
    com6 = {"lnab": np.tile(np.concatenate([g("L1_attn_ln_g"), g("L1_attn_ln_b")]), (128, 1)),
            "oWT": np.ascontiguousarray(g("L1_attn_oW").T),
            "w1s": w1s,
            "w2T": np.ascontiguousarray(g("L1_ffn_W2").T),
            "rw2T": np.ascontiguousarray(g("L1_ffn_router_W").T),
            "lng2c": np.ascontiguousarray(g("L1_ffn_ln_g").reshape(_D // 128, 128).T),
            "b2rep": np.tile(g("L1_ffn_ln_b")[:, None], (1, 128)),
            "rbB": np.tile(g("L1_ffn_router_b"), (128, 1)),
            "uT": np.ascontiguousarray(g("L1_ffn_raw_U").T),
            "vN": g("L1_ffn_raw_V"),
            "gammaB": np.tile(g("L1_ffn_gamma").reshape(_D), (128, 1)),
            "identb": eye16, "onesf": np.ones((128, 128), ml_dtypes.bfloat16)}
    in6 = []
    for i in range(_NC):
        if i % 2 == 1:
            shalf = np.asarray(r5[i - 1]["sfin"], np.float32)
            hgate = np.ones((128, 1), np.float32)
        else:
            shalf = np.zeros((128, _VD), np.float32)
            hgate = np.zeros((128, 1), np.float32)
        in6.append(dict(com6, obuf=r5[i]["obuf"], qeDT=r5[i]["qeDT"],
                        shalf=shalf, hgate=hgate, gbuf=r4[i]["gbuf"],
                        x1=r4[i]["x1"], stats=r4[i]["statso"], xn1T=r4[i]["xn1T"]))
    r6 = _launch(_build_p6, in6, ["out"])
    out = np.concatenate([np.asarray(r6[i]["out"], np.float32) for i in range(_NC)], axis=0)
    return out.reshape(_B, _S, _D)


_RG = [list(range(8))]


def _pbarrier(prog, *deps):
    for e in ENGINES:
        prog.barrier_wait(e, *deps)


def _build_fused():
    import contextlib
    nc = bass.Bass("TRN2", target_bir_lowering=False, debug=False,
                   detect_race_conditions=False)
    cfg = Cfg(D=_D)
    dr = _mk_dr(nc)
    x_my = dr("x_my", (_NTOK, _D))
    prevrow = dr("prevrow", (1, _D))
    gates = dr("gates", (128, 4))
    qcol = dr("qcol", (1, 1), mybir.dt.int32)
    qrow = dr("qrow", (1, 1), mybir.dt.int32)
    qoff64 = dr("qoff64", (1, 1), mybir.dt.int32)
    selv = dr("selv", (128, 8))
    hgate = dr("hgate", (128, 1))
    ident32 = dr("ident32", (128, 128))
    identb = dr("identb", (128, 128), BF16)
    onesf = dr("onesf", (128, 128), BF16)
    wqT = dr("wqT", (_D, _D), F32R)
    wkT = dr("wkT", (_D, _D), F32R)
    wvT = dr("wvT", (_D, 512), F32R)
    wpT = dr("wpT", (_D, _D), F32R)
    bprojB = dr("bprojB", (128, _D))
    rwT = dr("rwT", (_D, _E))
    rbB8 = dr("rbB8", (128, _E))
    WiS = dr("WiS", (_E, _D, _IS))
    WoS = dr("WoS", (_E, _IS, _D))
    onesb = dr("onesb", (128, _E * 128), BF16)
    wx0T = dr("wx0T", (_D, 5 * _PLR))
    w2a = dr("w2a", (_PLR + 1, 5 * _D))
    mu0c = dr("mu0c", (128, _D // 128))
    rWT = dr("rWT", (_D, _KD))
    kWT = dr("kWT", (_D, _KD))
    vWT = dr("vWT", (_D, _VD))
    wAT = dr("wAT", (_D, _GLR))
    wBTa = dr("wBTa", (_GLR + 1, _KD))
    gAT = dr("gAT", (_D, _GLR))
    gBTa = dr("gBTa", (_GLR + 1, _VD))
    mstk = dr("mstk", (128, 9 * 128 + 4), F32R)
    mkw = dr("mkw", (128, 128), F32R)
    msks = dr("msks", (128, 6 * 32))
    ucol = dr("ucol", (128, _NH))
    lnab = dr("lnab", (128, 2 * _VD))
    oWT = dr("oWT", (_VD, _D))
    w1s = dr("w1s", (_II, _D))
    w2T = dr("w2T", (_II, _D))
    rw2T = dr("rw2T", (_D, 512))
    lng2c = dr("lng2c", (128, _D // 128))
    b2rep = dr("b2rep", (_D, 128))
    rbB2 = dr("rbB2", (128, 512))
    uT = dr("uT", (_D, 512))
    vN = dr("vN", (512, _D))
    gammaB = dr("gammaB", (128, _D))
    out = dr("out", (_NTOK, _D), kind="ExternalOutput")
    idr = lambda nm, shp, dt=F32: nc.dram_tensor(nm, list(shp), dt).ap()
    xnT_loc = idr("xnTloc", (2 * _D, _NTOK // 2), BF16)
    xag = idr("xag", (2 * 8 * _D, _NTOK // 2), BF16)
    Rloc = idr("Rloc", (512, _D))
    Rall = idr("RallT", (8 * 512, _D))
    moeP = idr("moeP", (_NTOK, _D))
    moeS = idr("moeS", (512, _D))
    dloc = idr("dloc", (512, _D))
    dall = idr("dall", (8 * 512, _D))
    x1d = idr("x1d", (_NTOK, _D))
    statsd = idr("statsd", (_NTOK, 2))
    xn1Td = idr("xn1Td", (_D, _NTOK), BF16)
    rbd = idr("rbd", (_NTOK, _KD), BF16)
    kbd = idr("kbd", (_NTOK, _KD), BF16)
    wbd = idr("wbd", (_NTOK, _KD), F32R)
    vbd = idr("vbd", (_NTOK, _VD), BF16)
    gbd = idr("gbd", (_NTOK, _VD), BF16)
    obufd = idr("obufd", (_NTOK, _VD))
    qeDTd = idr("qeDTd", (128, _NH * _NTOK), BF16)
    sfind = idr("sfind", (128, _NH * _DV), F32R)
    sfall = idr("sfall", (8 * 128, _NH * _DV), F32R)

    with contextlib.ExitStack() as gstack:
        prog = Prog(nc)
        cc = prog.sem("cc")

        with contextlib.ExitStack() as st:
            bufs = alloc_ln_bufs(st, nc, cfg, "ln", 8, panel_dt=BF16)
            ldid = prog.sem("ldid")
            prog.op("y", lambda e: e.dma_start(out=bufs["ident"][:], in_=ident32[:, :]),
                    incs=[ldid.inc(16)])
            prog.barrier_wait("t", (ldid, 16))
            HWC = _NTOK // 2

            def p0_out(c, col0, ncols):
                h = col0 // HWC
                return xnT_loc[h * _D + c * 128: h * _D + c * 128 + 128,
                               col0 - h * HWC: col0 - h * HWC + ncols]
            d0, _ = emit_ln_transpose(prog, nc, cfg, x_my, xnT_loc, stats_dram=None,
                                      bufs=bufs, n_tok=_NTOK, panel_tiles=8,
                                      out_fn=p0_out)
        half_dma = d0.n // 2
        prog.op("g", lambda e: e.collective_compute(
                    "AllGather", ALU.bypass, replica_groups=_RG,
                    ins=[xnT_loc[0:_D, :]], outs=[xag[0:8 * _D, :]]),
                waits=[(d0, half_dma)], incs=[cc.inc()])
        prog.op("g", lambda e: e.collective_compute(
                    "AllGather", ALU.bypass, replica_groups=_RG,
                    ins=[xnT_loc[_D:2 * _D, :]], outs=[xag[8 * _D:16 * _D, :]]),
                waits=[(d0, d0.n)], incs=[cc.inc()])
        _pbarrier(prog, (d0, d0.n), (cc, 2))

        with contextlib.ExitStack() as st:
            d2 = emit_l0_attn(prog, nc, st, cfg, xag, qcol, Rloc,
                              wqT, wkT, wvT, wpT, bprojB, identb,
                              S0=_S, QT=1024, PT=1024, VW=512,
                              xag_kv=xag, qrow_dram=qrow)
        prog.op("g", lambda e: e.collective_compute(
                    "AllGather", ALU.bypass, replica_groups=_RG,
                    ins=[Rloc[:, :]], outs=[Rall[:, :]]),
                waits=[(d2, d2.n)], incs=[cc.inc()])
        _pbarrier(prog, (d2, d2.n))

        with contextlib.ExitStack() as st:
            HWC = _NTOK // 2

            def moe_x(c, t):
                h = (t * 512) // HWC
                row = h * 8 * _D + c * 128
                col = (t * 512) % HWC
                return xag[row:row + 128, col:col + 512]
            d1 = emit_moe(prog, nc, st, cfg, xag[0:_D, :], WiS, WoS, rwT, rbB8,
                          identb, onesb, moeP, SM=_NTOK, E=_E, IS=_IS, xsrc_fn=moe_x)
        prog.op("g", lambda e: e.collective_compute(
                    "ReduceScatter", ALU.add, replica_groups=_RG,
                    ins=[moeP[:, :]], outs=[moeS[:, :]]),
                waits=[(d1, d1.n)], incs=[cc.inc()])
        _pbarrier(prog, (d1, d1.n), (cc, 4))

        with contextlib.ExitStack() as st:
            d3 = emit_combine0(prog, nc, st, cfg, Rall, moeS, qoff64, dloc,
                               SL=512, NCLS=8)
        prog.op("g", lambda e: e.collective_compute(
                    "AllGather", ALU.bypass, replica_groups=_RG,
                    ins=[dloc[:, :]], outs=[dall[:, :]]),
                waits=[(d3, d3.n)], incs=[cc.inc()])
        _pbarrier(prog, (d3, d3.n), (cc, 5))

        with contextlib.ExitStack() as st:
            ds4 = emit_l1_prep(prog, nc, st, cfg, x_my, dall, prevrow, gates,
                               wx0T, w2a, mu0c, rWT, kWT, vWT, wAT, wBTa, gAT,
                               gBTa, identb, x1d, statsd, xn1Td,
                               rbd, kbd, wbd, vbd, gbd,
                               NTOK=_NTOK, KD=_KD, VD=_VD, GLR=_GLR, PLR=_PLR, PP=1024)
        _pbarrier(prog, *[(d, d.n) for d in ds4])

        with contextlib.ExitStack() as st:
            ds5 = emit_rwkv_main(prog, nc, st, cfg, rbd, kbd, wbd, vbd,
                                 mstk, mkw, msks, ucol, identb,
                                 obufd, qeDTd, sfind, NTOK=_NTOK, NH=_NH,
                                 DK=128, DV=_DV)
        prog.op("g", lambda e: e.collective_compute(
                    "AllGather", ALU.bypass, replica_groups=_RG,
                    ins=[sfind[:, :]], outs=[sfall[:, :]]),
                waits=[(ds5[2], ds5[2].n)], incs=[cc.inc()])
        _pbarrier(prog, *[(d, d.n) for d in ds5], (cc, 6))

        with contextlib.ExitStack() as st:
            d6 = emit_final(prog, nc, st, cfg, obufd, qeDTd, None, hgate,
                            gbd, lnab, oWT, x1d, statsd, xn1Td, w1s, w2T,
                            rw2T, lng2c, b2rep, rbB2, uT, vN, gammaB,
                            identb, onesf, out,
                            NTOK=_NTOK, NH=_NH, DV=_DV, NB=512, II=_II,
                            shalf_dyn=(sfall, selv))
        prog.barrier_wait("y", (d6, d6.n))
        with nc.Block() as block:
            prog.replay(block)
    return nc


def _stage_fused(inputs):
    import ml_dtypes
    g = lambda k: np.ascontiguousarray(np.asarray(inputs[k], np.float32))
    xf = g("L0_attn_x").reshape(_B * _S, _D)
    eye32 = np.eye(128, dtype=np.float32)
    eye16 = _eye16()
    wqkv = g("L0_attn_w_qkv")
    Wi = g("L0_ffn_experts_inter")
    Wo = g("L0_ffn_experts_out")
    sel = np.zeros((128, _E * 128), np.float32)
    for e in range(_E):
        sel[e, e * 128:(e + 1) * 128] = 1.0
    sel = sel.astype(ml_dtypes.bfloat16)
    Wx2 = g("L1_attn_Wx2")
    x_bias = g("L1_attn_x_bias")
    w2a = np.zeros((_PLR + 1, 5 * _D), np.float32)
    for n in range(5):
        w2a[:_PLR, n * _D:(n + 1) * _D] = Wx2.reshape(_D, 5, _PLR)[:, n].T
        w2a[_PLR, n * _D:(n + 1) * _D] = x_bias[n]
    wBTa = np.zeros((_GLR + 1, _KD), np.float32)
    wBTa[:_GLR] = g("L1_attn_wB").T
    wBTa[_GLR] = g("L1_attn_wb")
    gBTa = np.zeros((_GLR + 1, _VD), np.float32)
    gBTa[:_GLR] = g("L1_attn_gB").T
    gBTa[_GLR] = g("L1_attn_gb")
    MstackN, MkwN, masksT = build_rwkv_consts()
    W1 = g("L1_ffn_W1")
    w1sa = np.ascontiguousarray(
        W1.reshape(_II // 128, 128, _D // 128, 128).transpose(0, 3, 2, 1).reshape(_II, _D))
    com = {
        "ident32": eye32, "identb": eye16, "onesf": np.ones((128, 128), ml_dtypes.bfloat16),
        "wqT": np.ascontiguousarray(wqkv[0:1024].T),
        "wkT": np.ascontiguousarray(wqkv[1024:2048].T),
        "wvT": np.ascontiguousarray(wqkv[2048:2560].T),
        "wpT": np.ascontiguousarray(g("L0_attn_w_proj").T),
        "bprojB": np.tile(g("L0_attn_b_proj"), (128, 1)),
        "rwT": np.ascontiguousarray(g("L0_ffn_router_w").T),
        "rbB8": np.tile(g("L0_ffn_router_b"), (128, 1)),
        "onesb": sel,
        "wx0T": np.ascontiguousarray(g("L1_attn_Wx0").T),
        "w2a": w2a,
        "mu0c": np.ascontiguousarray(g("L1_attn_mu0").reshape(_D // 128, 128).T),
        "rWT": np.ascontiguousarray(g("L1_attn_rW").T),
        "kWT": np.ascontiguousarray(g("L1_attn_kW").T),
        "vWT": np.ascontiguousarray(g("L1_attn_vW").T),
        "wAT": np.ascontiguousarray(g("L1_attn_wA").T),
        "wBTa": wBTa,
        "gAT": np.ascontiguousarray(g("L1_attn_gA").T),
        "gBTa": gBTa,
        "mstk": MstackN, "mkw": MkwN, "msks": masksT,
        "ucol": np.ascontiguousarray(g("L1_attn_bonus").T),
        "lnab": np.tile(np.concatenate([g("L1_attn_ln_g"), g("L1_attn_ln_b")]), (128, 1)),
        "oWT": np.ascontiguousarray(g("L1_attn_oW").T),
        "w1s": w1sa,
        "w2T": np.ascontiguousarray(g("L1_ffn_W2").T),
        "rw2T": np.ascontiguousarray(g("L1_ffn_router_W").T),
        "lng2c": np.ascontiguousarray(g("L1_ffn_ln_g").reshape(_D // 128, 128).T),
        "b2rep": np.tile(g("L1_ffn_ln_b")[:, None], (1, 128)),
        "rbB2": np.tile(g("L1_ffn_router_b"), (128, 1)),
        "uT": np.ascontiguousarray(g("L1_ffn_raw_U").T),
        "vN": g("L1_ffn_raw_V"),
        "gammaB": np.tile(g("L1_ffn_gamma").reshape(_D), (128, 1)),
    }
    in_maps = []
    for i in range(_NC):
        gts = np.zeros((128, 4), np.float32)
        gts[:, 0] = 1.0 if i == 0 else 0.0
        gts[:, 1] = 1.0 if i == 1 else 0.0
        gts[:, 2] = 1.0 if i % 2 == 1 else 0.0
        pr = xf[i * _NTOK - 1:i * _NTOK] if i % 2 == 1 else np.zeros((1, _D), np.float32)
        hg = np.full((128, 1), 1.0 if i % 2 == 1 else 0.0, np.float32)
        selvi = np.zeros((128, 8), np.float32)
        if i % 2 == 1:
            selvi[:, i - 1] = 1.0
        in_maps.append(dict(
            com, x_my=xf[ts(i, _NTOK)],
            prevrow=np.ascontiguousarray(pr), gates=gts, hgate=hg,
            WiS=np.ascontiguousarray(Wi[:, :, ts(i, _IS)]),
            WoS=np.ascontiguousarray(Wo[:, ts(i, _IS), :]),
            qcol=np.array([[1024 * (i % 2)]], np.int32),
            qrow=np.array([[1024 * (((i % 4) // 2) * 8 + i // 4)]], np.int32),
            qoff64=np.array([[64 * i]], np.int32),
            selv=selvi))
    return in_maps


def kernel(**inputs):
    in_maps = _stage_fused(inputs)
    nc = _build_fused()
    res = run_bass_kernel_spmd(nc, in_maps, core_ids=list(range(_NC)))
    out = np.concatenate([np.asarray(res.results[i]["out"], np.float32)
                          for i in range(_NC)], axis=0)
    return out.reshape(_B, _S, _D)
